# version 25
# baseline (speedup 1.0000x reference)
import numpy as np
import ml_dtypes
import concourse.bass as bass
import concourse.mybir as mybir
from concourse.bass_utils import run_bass_kernel_spmd
from contextlib import ExitStack

F32 = mybir.dt.float32
BF16 = mybir.dt.bfloat16
AOT = mybir.AluOpType
AF = mybir.ActivationFunctionType

NH, HD, BLK = 16, 128, 256
D = 2048
EPS_RMS = 1e-6
EPS_LN = 1e-5


def window_blocks(npre_max):
    import math
    out = []
    for h in range(NH):
        s = 1.0 / 2.0 ** (8.0 * (h + 1) / 16.0)
        out.append(min(npre_max, max(1, math.ceil(120.0 / (256.0 * s)))))
    return out


def build_nc(T, debug=False, expose=False):
    NB = T // BLK
    TP = T
    NPRE = NB
    NPTC = TP // 128
    NCH = T // 256
    WB = window_blocks(NPRE)
    WTC = [2 * w for w in WB]
    WT0 = [TP - 256 * w for w in WB]

    nc = bass.Bass("TRN2", target_bir_lowering=False)

    xT = nc.dram_tensor("xT", [D, T], F32, kind="ExternalInput")
    xprevT = nc.dram_tensor("xprevT", [D, TP], F32, kind="ExternalInput")
    xnat = nc.dram_tensor("xnat", [T, D], F32, kind="ExternalInput")
    wqT = nc.dram_tensor("wqT", [D, D], F32, kind="ExternalInput")
    wkT = nc.dram_tensor("wkT", [D, D], F32, kind="ExternalInput")
    wvT = nc.dram_tensor("wvT", [D, D], F32, kind="ExternalInput")
    wgT = nc.dram_tensor("wgT", [D, D], F32, kind="ExternalInput")
    woT = nc.dram_tensor("woT", [D, D], F32, kind="ExternalInput")
    wfT = nc.dram_tensor("wfT", [D, D], F32, kind="ExternalInput")
    normw_c = nc.dram_tensor("normw_c", [128, 16], F32, kind="ExternalInput")
    scalerep = nc.dram_tensor("scalerep", [128, D], F32, kind="ExternalInput")
    biasrow = nc.dram_tensor("biasrow", [1, D], F32, kind="ExternalInput")
    sloperow = nc.dram_tensor("sloperow", [1, 16], F32, kind="ExternalInput")
    row256 = nc.dram_tensor("row256", [1, 256], F32, kind="ExternalInput")
    kcol2 = nc.dram_tensor("kcol2", [128, 2], F32, kind="ExternalInput")
    kcolpre = nc.dram_tensor("kcolpre", [128, NH * 32], F32, kind="ExternalInput")
    idxTc = nc.dram_tensor("idxTc", [128, 512], F32, kind="ExternalInput")
    maskTc = nc.dram_tensor("maskTc", [128, 512], F32, kind="ExternalInput")

    y = nc.dram_tensor("y", [T, D], F32, kind="ExternalOutput")

    qs = nc.dram_tensor("qs", [D, T], BF16, kind="Internal")
    kts = nc.dram_tensor("kts", [NH, T, HD], BF16, kind="Internal")
    vs = nc.dram_tensor("vs", [NH, 128, T], BF16, kind="Internal")
    kps = nc.dram_tensor("kps", [NH, 128, TP], BF16, kind="Internal")
    vps = nc.dram_tensor("vps", [NH, 128, TP], BF16, kind="Internal")
    gates = nc.dram_tensor("gates", [T // 256, 128, 16, 256], F32, kind="Internal")
    outs = nc.dram_tensor("outs", [T // 256, 128, 16, 256], BF16,
                      kind="ExternalOutput" if expose else "Internal")
    qdec_d = nc.dram_tensor("qdec_d", [16, 256], F32, kind="Internal")
    hsf = nc.dram_tensor("hsf", [T, D], F32,
                     kind="ExternalOutput" if expose else "Internal")
    hsb = nc.dram_tensor("hsb", [T, D], BF16,
                     kind="ExternalOutput" if expose else "Internal")
    if debug:
        d_r = nc.dram_tensor("d_r", [T, D], F32, kind="ExternalOutput")
        d_lnst = nc.dram_tensor("d_lnst", [128, 16], F32, kind="ExternalOutput")

    es = ExitStack()
    SEM = {}
    C = {}

    def sem(*names):
        for n in names:
            SEM[n] = es.enter_context(nc.semaphore(n))
            C[n] = 0

    def inc(n, k=1):
        C[n] += k
        return C[n]

    sem("w0", "w1", "st0", "st1", "st2", "st3", "qx",
        "lq0", "lq1", "lp", "tr0", "tr1", "os0", "os1", "kt0", "kt1",
        "lo0", "lo1", "lo2", "gg0", "gg1", "acc0", "acc1",
        "hs0", "hs1", "hs2", "hs3", "hs4", "hs5",
        "hb0", "hb1", "hb2", "hb3", "hb4", "hb5", "wob_s",
        "wf0", "wf1", "ht0", "ht1", "hf0", "hf1", "ys0", "ys1",
        "qg", "qs_", "qc", "misc",
        "mmT", "evT", "psc", "mmD", "evD",
        "bkq", "bakq", "bkqm", "boT", "baoT", "bct", "bkv", "bvq",
        "c1sq", "c1ss", "c1rsa", "c1rsv", "c1rp", "c1g", "c1mm", "c1r",
        "c1acc", "c1lnm", "c1lsq", "c1var", "c1sd", "c1ap",
        "c2w", "c2mm", "c2r", "c2lnm", "c2lsq", "c2var", "c2sd", "c2ap",
        "yst")

    ones1f = es.enter_context(nc.sbuf_tensor("ones1f", [1, 128], F32))
    onescol_b = es.enter_context(nc.sbuf_tensor("onescol_b", [128, 1], BF16))
    ones1b = es.enter_context(nc.sbuf_tensor("ones1b", [1, 128], BF16))
    slr = es.enter_context(nc.sbuf_tensor("slr", [1, 16], F32))
    cbank = es.enter_context(nc.sbuf_tensor("cbank", [128, 4], F32))
    sneg_rep = es.enter_context(nc.sbuf_tensor("sneg_rep", [128, 16], F32))
    kdpre_w = es.enter_context(nc.sbuf_tensor("kdpre_w", [128, NH * 32], F32))
    psm_es = ExitStack()
    p_small = psm_es.enter_context(nc.psum_tensor("p_small", [128, 256], F32))
    p_small2 = psm_es.enter_context(nc.psum_tensor("p_small2", [128, 256], F32))

    with nc.Block():
        pass

    with nc.Block() as blk:
        @blk.gpsimd
        def _(g):
            g.dma_start(out=slr[0:1, :], in_=sloperow[0:1, :]).then_inc(SEM["qg"], 16)
            inc("qg", 16)
            g.memset(ones1f[0:1, :], 1.0)
            g.memset(ones1b[0:1, :], 1.0)
            g.memset(onescol_b[:, 0:1], 1.0)
            g.memset(cbank[:, 0:1], float(EPS_LN))
            g.memset(cbank[:, 1:2], float(EPS_RMS))
            g.memset(cbank[:, 2:3], float(1.0 / D))
            g.wait_ge(SEM["qg"], C["qg"])

    with nc.Block() as blk:
        @blk.vector
        def _(v):
            v.tensor_scalar_mul(slr[0:1, :], slr[0:1, :], -1.0)

    with nc.Block() as blk:
        @blk.tensor
        def _(t):
            t.matmul(p_small[:, 0:16], ones1f[0:1, :], slr[0:1, :],
                     start=True, stop=True)

    with nc.Block() as blk:
        @blk.scalar
        def _(s):
            s.copy(sneg_rep[:, :], p_small[:, 0:16])

    kcol_stage = es.enter_context(nc.sbuf_tensor("kcol_stage", [128, NH * 32], F32))
    with nc.Block() as blk:
        @blk.gpsimd
        def _(g):
            g.dma_start(out=kcol_stage[:, :], in_=kcolpre[:, :]).then_inc(SEM["qg"], 16)
            inc("qg", 16)
            g.wait_ge(SEM["qg"], C["qg"])

    with nc.Block() as blk:
        @blk.vector
        def _(v):
            v.tensor_tensor(
                kdpre_w[:, :].rearrange("p (h tc) -> p h tc", h=NH),
                sneg_rep[:, :].unsqueeze(2).broadcast_to([128, NH, 32]),
                kcol_stage[:, :].rearrange("p (h tc) -> p h tc", h=NH),
                AOT.mult)

    with nc.Block() as blk:
        @blk.scalar
        def _(s):
            s.activation(kdpre_w[:, :], kdpre_w[:, :], AF.Exp)

    def emit_projT(blk, xbuf, xlen, groups, prescale, psT, stagT, wkvb):
        mm0, ev0, ps0 = C["mmT"], C["evT"], C["psc"]
        ng = len(groups)
        cum = [0]
        for g_ in groups:
            cum.append(cum[-1] + g_[7])
        wmark = [inc("w%d" % (gi % 2), 32) for gi in range(ng)]
        smark = [inc("st%d" % (gi % 2), 32) for gi in range(ng)]

        @blk.gpsimd
        def _(g):
            for gi, grp in enumerate(groups):
                wA, cA, wB, cB = grp[0], grp[1], grp[2], grp[3]
                if gi >= 2:
                    g.wait_ge(SEM["mmT"], mm0 + cum[gi - 1])
                sl = (gi % 2) * (16 * 256)
                wv = wkvb[:, sl:sl + 16 * 256].rearrange("p (a c) -> p a c", a=16)
                g.dma_start(out=wv[:, :, 0:128],
                            in_=wA[:, cA:cA + 128].rearrange("(a p) c -> p a c", p=128)
                            ).then_inc(SEM["w%d" % (gi % 2)], 16)
                g.dma_start(out=wv[:, :, 128:256],
                            in_=wB[:, cB:cB + 128].rearrange("(a p) c -> p a c", p=128)
                            ).then_inc(SEM["w%d" % (gi % 2)], 16)

        @blk.tensor
        def _(t):
            for gi, grp in enumerate(groups):
                tok0, ntc = grp[6], grp[7]
                t.wait_ge(SEM["w%d" % (gi % 2)], wmark[gi])
                sl = (gi % 2) * (16 * 256)
                for tc in range(ntc):
                    gidx = cum[gi] + tc
                    if gidx >= 2:
                        t.wait_ge(SEM["evT"], ev0 + gidx - 1)
                    pt = psT[gidx % 2]
                    for k in range(16):
                        ins = t.matmul(
                            pt[:, :],
                            xbuf[:, k * xlen + tok0 + tc * 128:
                                 k * xlen + tok0 + tc * 128 + 128],
                            wkvb[:, sl + k * 256: sl + k * 256 + 256],
                            start=(k == 0), stop=(k == 15))
                    ins.then_inc(SEM["mmT"], 1)
                    inc("mmT")

        @blk.scalar
        def _(s):
            for gi, grp in enumerate(groups):
                ntc = grp[7]
                for tc in range(ntc):
                    gidx = cum[gi] + tc
                    s.wait_ge(SEM["mmT"], mm0 + gidx + 1)
                    if gi >= 2 and tc == 0:
                        s.wait_ge(SEM["st%d" % (gi % 2)], smark[gi - 2])
                    st = stagT[gi % 2]
                    s.activation(st[:, tc * 256:(tc + 1) * 256],
                                 psT[gidx % 2][:, :], AF.Silu).then_inc(SEM["evT"], 1)
                    inc("evT")

        if prescale is not None:
            @blk.vector
            def _(v):
                for gi, grp in enumerate(groups):
                    h = prescale[gi]
                    ntc = grp[7]
                    st = stagT[gi % 2]
                    for tc in range(ntc):
                        gidx = cum[gi] + tc
                        v.wait_ge(SEM["evT"], ev0 + gidx + 1)
                        v.tensor_scalar_mul(
                            st[:, tc * 256: tc * 256 + 128],
                            st[:, tc * 256: tc * 256 + 128],
                            kdpre_w[:, h * 32 + tc: h * 32 + tc + 1],
                        ).then_inc(SEM["psc"], 1)
                        inc("psc")

        @blk.sync
        def _(sy):
            for gi, grp in enumerate(groups):
                spillA, spillB, ntc = grp[4], grp[5], grp[7]
                if prescale is not None:
                    sy.wait_ge(SEM["psc"], ps0 + cum[gi + 1])
                else:
                    sy.wait_ge(SEM["evT"], ev0 + cum[gi + 1])
                st = stagT[gi % 2]
                stv = st[:, 0:ntc * 256].rearrange("p (tc c) -> p tc c", c=256)

                def spill_out(sp):
                    if sp.shape[0] == 128:
                        return sp[:, 0:ntc * 128].rearrange(
                            "p (tc d) -> p tc d", d=128)
                    return sp[0:ntc * 128].rearrange("(tc p) d -> p tc d", p=128)

                sy.dma_start(out=spill_out(spillA),
                             in_=stv[:, :, 0:128]).then_inc(SEM["st%d" % (gi % 2)], 16)
                sy.dma_start(out=spill_out(spillB),
                             in_=stv[:, :, 128:256]).then_inc(SEM["st%d" % (gi % 2)], 16)
            sy.wait_ge(SEM["st0"], C["st0"])
            sy.wait_ge(SEM["st1"], C["st1"])

    def emit_projD(blk, xbuf, xlen, mcs, psD, evb, evf, wdb):
        ng = xlen // 512
        mm0, ev0 = C["mmD"], C["evD"]
        nm = len(mcs)
        wmark = [inc("w%d" % (gi % 2), 16) for gi in range(nm)]
        smark = [inc("st%d" % (gidx % 4), 16) for gidx in range(nm * ng)]

        @blk.gpsimd
        def _(g):
            for gi, (wsrc, c0, _sp, _a, _dt) in enumerate(mcs):
                if gi >= 2:
                    g.wait_ge(SEM["mmD"], mm0 + ng * (gi - 1))
                sl = (gi % 2) * (16 * 128)
                g.dma_start(out=wdb[:, sl:sl + 16 * 128].rearrange(
                                "p (a c) -> p a c", a=16),
                            in_=wsrc[:, c0:c0 + 128].rearrange("(a p) c -> p a c", p=128)
                            ).then_inc(SEM["w%d" % (gi % 2)], 16)

        @blk.tensor
        def _(t):
            for gi in range(nm):
                t.wait_ge(SEM["w%d" % (gi % 2)], wmark[gi])
                sl = (gi % 2) * (16 * 128)
                for n in range(ng):
                    gidx = gi * ng + n
                    if gidx >= 2:
                        t.wait_ge(SEM["evD"], ev0 + gidx - 1)
                    pd = psD[gidx % 2]
                    for k in range(16):
                        ins = t.matmul(
                            pd[:, :],
                            wdb[:, sl + k * 128: sl + k * 128 + 128],
                            xbuf[:, k * xlen + n * 512: k * xlen + n * 512 + 512],
                            start=(k == 0), stop=(k == 15))
                    ins.then_inc(SEM["mmD"], 1)
                    inc("mmD")

        @blk.scalar
        def _(s):
            for gi, (_w, _c0, _sp, act, odt) in enumerate(mcs):
                for n in range(ng):
                    gidx = gi * ng + n
                    s.wait_ge(SEM["mmD"], mm0 + gidx + 1)
                    if gidx >= 4:
                        s.wait_ge(SEM["st%d" % (gidx % 4)], smark[gidx - 4])
                    ev = evb[gidx % 4] if odt == BF16 else evf[gidx % 4]
                    s.activation(ev[:, :], psD[gidx % 2][:, :], act).then_inc(
                        SEM["evD"], 1)
                    inc("evD")

        @blk.sync
        def _(sy):
            for gi, (_w, c0, spill, _a, odt) in enumerate(mcs):
                for n in range(ng):
                    gidx = gi * ng + n
                    sy.wait_ge(SEM["evD"], ev0 + gidx + 1)
                    ev = evb[gidx % 4] if odt == BF16 else evf[gidx % 4]
                    if spill is None or len(spill.shape) == 4:
                        sy.dma_start(
                            out=spill[2 * n:2 * n + 2, :, c0 // 128, :]
                            .rearrange("c p t -> p c t"),
                            in_=ev[:, :].rearrange("p (c t) -> p c t", c=2)
                        ).then_inc(SEM["st%d" % (gidx % 4)], 16)
                    else:
                        sy.dma_start(out=spill[c0:c0 + 128, n * 512:(n + 1) * 512],
                                     in_=ev[:, :]).then_inc(
                            SEM["st%d" % (gidx % 4)], 16)
            for i in range(4):
                sy.wait_ge(SEM["st%d" % i], C["st%d" % i])

    with ExitStack() as A:
        xbuf = A.enter_context(nc.sbuf_tensor("xbuf", [128, 16 * T], BF16))
        wkvb = A.enter_context(nc.sbuf_tensor("wkvb", [128, 2 * 16 * 256], BF16))
        wdb = A.enter_context(nc.sbuf_tensor("wdb", [128, 2 * 16 * 128], BF16))
        stg0 = A.enter_context(nc.sbuf_tensor("stg0", [128, (T // 128) * 256], BF16))
        stg1 = A.enter_context(nc.sbuf_tensor("stg1", [128, (T // 128) * 256], BF16))
        evb = [A.enter_context(nc.sbuf_tensor(f"evb{i}", [128, 512], BF16))
               for i in range(4)]
        evf = [A.enter_context(nc.sbuf_tensor(f"evf{i}", [128, 512], F32))
               for i in range(4)]
        psT = [A.enter_context(nc.psum_tensor(f"psT{i}", [128, 256], F32))
               for i in range(2)]
        psD = [A.enter_context(nc.psum_tensor(f"psD{i}", [128, 512], F32))
               for i in range(2)]
        stagT = [stg0, stg1]

        with nc.Block() as blk:
            @blk.gpsimd
            def _(g):
                for fc in range(16):
                    g.dma_start(out=xbuf[:, fc * TP:(fc + 1) * TP],
                                in_=xprevT[fc * 128:(fc + 1) * 128, :]
                                ).then_inc(SEM["qx"], 16)
                    inc("qx", 16)
                g.wait_ge(SEM["qx"], C["qx"])

        with nc.Block() as blk:
            emit_projT(blk, xbuf, TP,
                       [(wkT, h * 128, wvT, h * 128, kps[h], vps[h],
                         WT0[h], WTC[h]) for h in range(NH)],
                       prescale=list(range(NH)), psT=psT, stagT=stagT, wkvb=wkvb)

        with nc.Block() as blk:
            @blk.gpsimd
            def _(g):
                for fc in range(16):
                    g.dma_start(out=xbuf[:, fc * T:(fc + 1) * T],
                                in_=xT[fc * 128:(fc + 1) * 128, :]
                                ).then_inc(SEM["qx"], 16)
                    inc("qx", 16)
                g.wait_ge(SEM["qx"], C["qx"])

        with nc.Block() as blk:
            emit_projD(blk, xbuf, T,
                       [(wqT, mc * 128, qs, AF.Silu, BF16) for mc in range(16)],
                       psD=psD, evb=evb, evf=evf, wdb=wdb)
        with nc.Block() as blk:
            emit_projT(blk, xbuf, T,
                       [(wkT, h * 128, wvT, h * 128, kts[h], vs[h],
                         0, T // 128) for h in range(NH)],
                       prescale=None, psT=psT, stagT=stagT, wkvb=wkvb)
        with nc.Block() as blk:
            emit_projD(blk, xbuf, T,
                       [(wgT, mc * 128, gates, AF.Sigmoid, F32) for mc in range(16)],
                       psD=psD, evb=evb, evf=evf, wdb=wdb)

    with ExitStack() as B:
        qdecr = B.enter_context(nc.sbuf_tensor("qdecr", [128, NH * 256], F32))
        kdTt = B.enter_context(nc.sbuf_tensor("kdTt", [128, NH * 2], F32))
        bdr = B.enter_context(nc.sbuf_tensor("bdr", [128, NH], F32))
        ddt = B.enter_context(nc.sbuf_tensor("ddt", [128, NH * 512], BF16))
        tw0 = B.enter_context(nc.sbuf_tensor("tw0", [128, 512], F32))
        tw1 = B.enter_context(nc.sbuf_tensor("tw1", [128, 512], F32))
        tw2 = B.enter_context(nc.sbuf_tensor("tw2", [128, 512], F32))
        r16 = B.enter_context(nc.sbuf_tensor("r16", [16, 257], F32))
        qrow = B.enter_context(nc.sbuf_tensor("qrow", [1, 16 * 256], F32))
        qbb = B.enter_context(nc.sbuf_tensor("qbb", [128, 2 * T], BF16))
        kdbb = B.enter_context(nc.sbuf_tensor("kdbb", [128, 2 * T], BF16))
        ktbb = B.enter_context(nc.sbuf_tensor("ktbb", [128, 2 * T], BF16))
        vtbb = B.enter_context(nc.sbuf_tensor("vtbb", [128, 2 * T], BF16))
        kpbb = B.enter_context(nc.sbuf_tensor("kpbb", [128, TP], BF16))
        vpbb = B.enter_context(nc.sbuf_tensor("vpbb", [128, TP], BF16))
        qdbb = B.enter_context(nc.sbuf_tensor("qdbb", [128, 2 * T], BF16))
        kqmb = B.enter_context(nc.sbuf_tensor("kqmb", [128, 4 * 512], BF16))
        kqeb = B.enter_context(nc.sbuf_tensor("kqeb", [128, 4 * 512], BF16))
        kvf2 = B.enter_context(nc.sbuf_tensor("kvf2", [128, 2 * 128], F32))
        kvb4 = B.enter_context(nc.sbuf_tensor("kvb4", [128, 4 * 128], BF16))
        ostg0 = B.enter_context(nc.sbuf_tensor("ostg0", [128, T], BF16))
        ostg1 = B.enter_context(nc.sbuf_tensor("ostg1", [128, T], BF16))
        ostg = [ostg0, ostg1]

        with nc.Block() as blk:
            @blk.gpsimd
            def _(g):
                g.dma_start(out=tw0[:, 0:2], in_=kcol2[:, :]).then_inc(SEM["qg"], 16)
                inc("qg", 16)
                g.dma_start(out=tw1[:, 0:512], in_=idxTc[:, :]).then_inc(SEM["qg"], 16)
                inc("qg", 16)
                g.dma_start(out=tw2[:, 0:512], in_=maskTc[:, :]).then_inc(SEM["qg"], 16)
                inc("qg", 16)
                g.wait_ge(SEM["qg"], C["qg"])

            @blk.sync
            def _(sy):
                sy.dma_start(out=r16[0:16, 256:257],
                             in_=sloperow[0:1, :].rearrange("one h -> h one")
                             ).then_inc(SEM["qs_"], 16)
                inc("qs_", 16)
                sy.wait_ge(SEM["qs_"], C["qs_"])

        with nc.Block() as blk:
            @blk.vector
            def _(v):
                v.tensor_tensor(
                    kdTt[:, :].rearrange("p (h t) -> p h t", h=NH),
                    sneg_rep[:, :].unsqueeze(2).broadcast_to([128, NH, 2]),
                    tw0[:, 0:2].unsqueeze(1).broadcast_to([128, NH, 2]),
                    AOT.mult)
                v.tensor_scalar_mul(bdr[:, :], sneg_rep[:, :], 256.0)
                v.tensor_scalar_mul(r16[0:16, 256:257], r16[0:16, 256:257], -1.0)

        with nc.Block() as blk:
            @blk.scalar
            def _(s):
                s.activation(kdTt[:, :], kdTt[:, :], AF.Exp)
                s.activation(bdr[:, :], bdr[:, :], AF.Exp)

        with nc.Block() as blk:
            @blk.gpsimd
            def _(g):
                g.dma_start(out=qrow[0:1, 0:256], in_=row256[0:1, :]).then_inc(SEM["qg"], 16)
                inc("qg", 16)
                g.wait_ge(SEM["qg"], C["qg"])

        with nc.Block() as blk:
            @blk.tensor
            def _(t):
                t.matmul(p_small[0:16, 0:256], ones1f[0:1, 0:16], qrow[0:1, 0:256],
                         start=True, stop=True)

        with nc.Block() as blk:
            @blk.scalar
            def _(s):
                s.activation(r16[0:16, 0:256], p_small[0:16, 0:256], AF.Exp,
                             scale=r16[0:16, 256:257])

        with nc.Block() as blk:
            @blk.sync
            def _(sy):
                sy.dma_start(out=qdec_d[:, :],
                             in_=r16[0:16, 0:256]).then_inc(SEM["qs_"], 16)
                inc("qs_", 16)
                sy.wait_ge(SEM["qs_"], C["qs_"])

        with nc.Block() as blk:
            @blk.gpsimd
            def _(g):
                g.dma_start(out=qrow[0:1, :],
                            in_=qdec_d[:, :].rearrange("h m -> (h m)")[None, :]
                            ).then_inc(SEM["qg"], 16)
                inc("qg", 16)
                g.wait_ge(SEM["qg"], C["qg"])

        with nc.Block() as blk:
            m0 = C["misc"]

            @blk.tensor
            def _(t):
                for h in range(NH):
                    if h >= 2:
                        t.wait_ge(SEM["misc"], m0 + h - 1)
                    ps = p_small if h % 2 == 0 else p_small2
                    t.matmul(ps[:, 0:256], ones1f[0:1, :],
                             qrow[0:1, h * 256:(h + 1) * 256],
                             start=True, stop=True).then_inc(SEM["bkq"], 1)
                    inc("bkq")

            @blk.scalar
            def _(s):
                for h in range(NH):
                    s.wait_ge(SEM["bkq"], C["bkq"] - NH + h + 1)
                    ps = p_small if h % 2 == 0 else p_small2
                    s.copy(qdecr[:, h * 256:(h + 1) * 256], ps[:, 0:256]).then_inc(
                        SEM["misc"], 1)
                    inc("misc")

        for h in range(NH):
            with nc.Block() as blk:
                @blk.vector
                def _(v, h=h):
                    v.tensor_scalar(tw0[:, 0:512], tw1[:, 0:512],
                                    sneg_rep[:, h:h + 1], 0.0,
                                    AOT.mult, AOT.min)

                @blk.scalar
                def _(s, h=h):
                    pass
            with nc.Block() as blk:
                @blk.scalar
                def _(s, h=h):
                    s.activation(tw0[:, 0:512], tw0[:, 0:512], AF.Exp)
            with nc.Block() as blk:
                @blk.vector
                def _(v, h=h):
                    v.tensor_tensor(ddt[:, h * 512:(h + 1) * 512], tw0[:, 0:512],
                                    tw2[:, 0:512], AOT.mult)

        psm_es.close()
        pkq = [B.enter_context(nc.psum_tensor(f"pkq{i}", [128, 512], F32))
               for i in range(3)]
        po = [B.enter_context(nc.psum_tensor(f"po{i}", [128, 256], F32))
              for i in range(3)]
        pcx = [B.enter_context(nc.psum_tensor(f"pc{i}", [128, 128], F32))
               for i in range(2)]

        HB = max(NB // 2, 1)
        order = []
        for p in range(NH + 1):
            for j in range(HB):
                for (h_, i_) in ((p - 1, j + HB), (p, j)):
                    if 0 <= h_ < NH and i_ < NB:
                        order.append((h_, i_))
        NOR = len(order)
        assert NOR == NH * NB
        pos = {hi: n for n, hi in enumerate(order)}
        LA = 2

        pe_stream = []
        _touched = set()

        def _pe_touch(m):
            hh, ii = order[m]
            if ii == 0 and hh not in _touched:
                _touched.add(hh)
                pe_stream.append(("ldw", hh))
            pe_stream.append(("kq", m))

        for m in range(min(LA, NOR)):
            _pe_touch(m)
        for n in range(NOR):
            if n + LA < NOR:
                _pe_touch(n + LA)
            h_, i_ = order[n]
            if i_ == 0:
                pe_stream.append(("pre", h_))
            pe_stream.append(("oT", n))
            if i_ < NB - 1:
                pe_stream.append(("ct", n))

        pe_ct_seq = []
        for op in pe_stream:
            if op[0] == "pre":
                pe_ct_seq.append(("pre", op[1]))
            elif op[0] == "ct":
                pe_ct_seq.append(("m",) + order[op[1]])
        pe_ct_idx = {cid: k for k, cid in enumerate(pe_ct_seq)}

        dve_stream = []
        _kqm_emitted = 0
        for n in range(NOR):
            h_, i_ = order[n]
            if i_ == 0:
                dve_stream.append(("prep", h_))
                dve_stream.append(("kvinit", h_))
            while _kqm_emitted <= min(n + 2, NOR - 1):
                dve_stream.append(("kqm", _kqm_emitted))
                _kqm_emitted += 1
            if i_ < NB - 1:
                dve_stream.append(("upd", n))
        dve_kv_ready = {}
        dve_ct_cons = {}
        _bkv = 0
        for op in dve_stream:
            if op[0] == "kvinit":
                _bkv += 1
                dve_kv_ready[(op[1], 0)] = _bkv
                dve_ct_cons[("pre", op[1])] = _bkv
            elif op[0] == "upd":
                h_, i_ = order[op[1]]
                _bkv += 1
                dve_kv_ready[(h_, i_ + 1)] = _bkv
                dve_ct_cons[("m", h_, i_)] = _bkv

        B0 = {k: C[k] for k in ["bkq", "bakq", "bkqm",
                                "boT", "baoT", "bct", "bkv", "bvq"]}
        ld_mark = {h: inc("lq%d" % (h % 2), 32) for h in range(NH)}
        gp_mark = {h: inc("lp", 32) for h in range(NH)}
        tr_mark = {h: inc("tr%d" % (h % 2), 16) for h in range(NH)}
        kt_mark = {h: inc("kt%d" % (h % 2), 16) for h in range(NH)}
        st_mark = {h: inc("os%d" % (h % 2), 16) for h in range(NH)}
        sync_ops = [("T", 0), ("K", 0), ("T", 1), ("K", 1)]
        for h in range(NH):
            if h + 2 < NH:
                sync_ops.append(("T", h + 2))
                sync_ops.append(("K", h + 2))
            sync_ops.append(("S", h))

        def kvslot(h, i):
            return ((h % 2) * 2 + (i % 2)) * 128

        with nc.Block() as blk:
            @blk.scalar
            def _(s):
                def issue_loads(h):
                    sl = (h % 2)
                    sm = SEM["lq%d" % sl]
                    s.dma_start(out=qbb[:, sl * T:sl * T + T],
                                in_=qs[h * 128:(h + 1) * 128, :]).then_inc(sm, 16)
                    s.dma_start(out=vtbb[:, sl * T:sl * T + T],
                                in_=vs[h]).then_inc(sm, 16)

                issue_loads(0)
                issue_loads(1)

                def kqe(m):
                    s.wait_ge(SEM["bkq"], B0["bkq"] + m + 1)
                    if m >= 4:
                        s.wait_ge(SEM["bkqm"], B0["bkqm"] + m - 3)
                    s.copy(kqeb[:, (m % 4) * 512:(m % 4) * 512 + 512],
                           pkq[m % 3][:, :]).then_inc(SEM["bakq"], 1)
                    inc("bakq")

                for m0 in range(min(2, NOR)):
                    kqe(m0)
                for n in range(NOR):
                    if n + 2 < NOR:
                        kqe(n + 2)
                    h, i = order[n]
                    s.wait_ge(SEM["boT"], B0["boT"] + n + 1)
                    if i == 0 and h >= 2:
                        s.wait_ge(SEM["os%d" % (h % 2)], st_mark[h - 2])
                    s.copy(ostg[h % 2][:, i * 256:(i + 1) * 256],
                           po[n % 3][:, :]).then_inc(SEM["baoT"], 1)
                    inc("baoT")
                    if i == NB - 1 and h + 2 < NH:
                        issue_loads(h + 2)

            @blk.gpsimd
            def _(g):
                for h in range(NH):
                    if h >= 1:
                        g.wait_ge(SEM["bct"],
                                  B0["bct"] + pe_ct_idx[("pre", h - 1)] + 1)
                    g.dma_start(out=kpbb[:, 0:WTC[h] * 128],
                                in_=kps[h][:, 0:WTC[h] * 128]).then_inc(SEM["lp"], 16)
                    g.dma_start(out=vpbb[:, 0:WTC[h] * 128],
                                in_=vps[h][:, 0:WTC[h] * 128]).then_inc(SEM["lp"], 16)

            @blk.sync
            def _(sy):
                for op, h in sync_ops:
                    if op == "T":
                        if h >= 2:
                            sy.wait_ge(SEM["bkq"],
                                       B0["bkq"] + pos[(h - 2, NB - 1)] + 1)
                        sy.dma_start_transpose(kdbb[:, (h % 2) * T:(h % 2) * T + T],
                                               kts[h]).then_inc(
                            SEM["tr%d" % (h % 2)], 16)
                    elif op == "K":
                        if h >= 2:
                            sy.wait_ge(SEM["boT"],
                                       B0["boT"] + pos[(h - 2, NB - 1)] + 1)
                        sy.dma_start(out=ktbb[:, (h % 2) * T:(h % 2) * T + T]
                                     .rearrange("p (tc d) -> p tc d", d=128),
                                     in_=kts[h].rearrange("(tc p) d -> p tc d", p=128)
                                     ).then_inc(SEM["kt%d" % (h % 2)], 16)
                    else:
                        sy.wait_ge(SEM["baoT"], B0["baoT"] + pos[(h, NB - 1)] + 1)
                        sy.dma_start(out=outs[:, :, h, :].rearrange("c p t -> p c t"),
                                     in_=ostg[h % 2][:, :].rearrange(
                                         "p (c t) -> p c t", t=256)).then_inc(
                            SEM["os%d" % (h % 2)], 16)
                for smn in ["tr0", "tr1", "os0", "os1", "kt0", "kt1"]:
                    sy.wait_ge(SEM[smn], C[smn])

            @blk.vector
            def _(v):
                for op in dve_stream:
                    if op[0] == "prep":
                        h = op[1]
                        sl = h % 2
                        v.wait_ge(SEM["lq%d" % sl], ld_mark[h])
                        v.wait_ge(SEM["kt%d" % sl], kt_mark[h])
                        if h >= 2:
                            v.wait_ge(SEM["boT"],
                                      B0["boT"] + pos[(h - 2, NB - 1)] + 1)
                        v.tensor_tensor(
                            qdbb[:, sl * T:sl * T + T].rearrange(
                                "p (r m) -> p r m", m=256),
                            qbb[:, sl * T:sl * T + T].rearrange(
                                "p (r m) -> p r m", m=256),
                            qdecr[:, h * 256:(h + 1) * 256].unsqueeze(1)
                            .broadcast_to([128, NB, 256]),
                            AOT.mult)
                        for half in range(2):
                            ins_v = v.tensor_tensor(
                                ktbb[:, sl * T:sl * T + T].rearrange(
                                    "p (tcp two d) -> p tcp two d", two=2, d=128
                                )[:, :, half, :],
                                ktbb[:, sl * T:sl * T + T].rearrange(
                                    "p (tcp two d) -> p tcp two d", two=2, d=128
                                )[:, :, half, :],
                                kdTt[:, h * 2 + half:h * 2 + half + 1]
                                .unsqueeze(2).broadcast_to([128, NB, 128]),
                                AOT.mult)
                        ins_v.then_inc(SEM["bvq"], 1)
                        inc("bvq")
                    elif op[0] == "kvinit":
                        h = op[1]
                        k_ct = pe_ct_idx[("pre", h)]
                        v.wait_ge(SEM["bct"], B0["bct"] + k_ct + 1)
                        v.tensor_copy(kvf2[:, (h % 2) * 128:(h % 2) * 128 + 128],
                                      pcx[k_ct % 2][:, :])
                        v.tensor_copy(kvb4[:, kvslot(h, 0):kvslot(h, 0) + 128],
                                      kvf2[:, (h % 2) * 128:(h % 2) * 128 + 128]
                                      ).then_inc(SEM["bkv"], 1)
                        inc("bkv")
                    elif op[0] == "kqm":
                        m = op[1]
                        h, i = order[m]
                        v.wait_ge(SEM["bakq"], B0["bakq"] + m + 1)
                        if m >= 4:
                            v.wait_ge(SEM["boT"], B0["boT"] + m - 3)
                        v.tensor_tensor(kqmb[:, (m % 4) * 512:(m % 4) * 512 + 512],
                                        kqeb[:, (m % 4) * 512:(m % 4) * 512 + 512],
                                        ddt[:, h * 512:(h + 1) * 512],
                                        AOT.mult).then_inc(SEM["bkqm"], 1)
                        inc("bkqm")
                    else:
                        n = op[1]
                        h, i = order[n]
                        k_ct = pe_ct_idx[("m", h, i)]
                        v.wait_ge(SEM["bct"], B0["bct"] + k_ct + 1)
                        if i >= 1:
                            v.wait_ge(SEM["boT"], B0["boT"] + pos[(h, i - 1)] + 1)
                        kf = kvf2[:, (h % 2) * 128:(h % 2) * 128 + 128]
                        v.scalar_tensor_tensor(kf, kf, bdr[:, h:h + 1],
                                               pcx[k_ct % 2][:, :],
                                               AOT.mult, AOT.add)
                        sl2 = kvslot(h, i + 1)
                        v.tensor_copy(kvb4[:, sl2:sl2 + 128], kf).then_inc(
                            SEM["bkv"], 1)
                        inc("bkv")

            @blk.tensor
            def _(t):
                for op in pe_stream:
                    if op[0] == "ldw":
                        h = op[1]
                        t.wait_ge(SEM["lq%d" % (h % 2)], ld_mark[h])
                        t.wait_ge(SEM["tr%d" % (h % 2)], tr_mark[h])
                        t.wait_ge(SEM["kt%d" % (h % 2)], kt_mark[h])
                    elif op[0] == "kq":
                        m = op[1]
                        h, i = order[m]
                        sl = (h % 2)
                        if m >= 3:
                            t.wait_ge(SEM["bakq"], B0["bakq"] + m - 2)
                        pk = pkq[m % 3]
                        t.matmul(pk[:, 0:256],
                                 kdbb[:, sl * T + i * 256: sl * T + i * 256 + 128],
                                 qbb[:, sl * T + i * 256: sl * T + i * 256 + 256],
                                 start=True, stop=True)
                        t.matmul(pk[:, 256:512],
                                 kdbb[:, sl * T + i * 256 + 128:
                                      sl * T + i * 256 + 256],
                                 qbb[:, sl * T + i * 256: sl * T + i * 256 + 256],
                                 start=True, stop=True).then_inc(SEM["bkq"], 1)
                        inc("bkq")
                    elif op[0] == "pre":
                        h = op[1]
                        t.wait_ge(SEM["bvq"], B0["bvq"] + h + 1)
                        t.wait_ge(SEM["lp"], gp_mark[h])
                        k_ct = pe_ct_idx[("pre", h)]
                        if k_ct >= 2:
                            t.wait_ge(SEM["bkv"],
                                      B0["bkv"] + dve_ct_cons[pe_ct_seq[k_ct - 2]])
                        pcb = pcx[k_ct % 2]
                        for ptc in range(WTC[h]):
                            ins = t.matmul(
                                pcb[:, :],
                                kpbb[:, ptc * 128: ptc * 128 + 128],
                                vpbb[:, ptc * 128: ptc * 128 + 128],
                                start=(ptc == 0), stop=(ptc == WTC[h] - 1))
                        ins.then_inc(SEM["bct"], 1)
                        inc("bct")
                    elif op[0] == "oT":
                        n = op[1]
                        h, i = order[n]
                        sl = (h % 2)
                        if n >= 3:
                            t.wait_ge(SEM["baoT"], B0["baoT"] + n - 2)
                        t.wait_ge(SEM["bkqm"], B0["bkqm"] + n + 1)
                        t.wait_ge(SEM["bkv"], B0["bkv"] + dve_kv_ready[(h, i)])
                        pob = po[n % 3]
                        sl4 = (n % 4) * 512
                        ks = kvslot(h, i)
                        t.matmul(pob[:, :],
                                 vtbb[:, sl * T + i * 256: sl * T + i * 256 + 128],
                                 kqmb[:, sl4:sl4 + 256], start=True, stop=False)
                        t.matmul(pob[:, :],
                                 vtbb[:, sl * T + i * 256 + 128:
                                      sl * T + i * 256 + 256],
                                 kqmb[:, sl4 + 256:sl4 + 512],
                                 start=False, stop=False)
                        t.matmul(pob[:, :],
                                 kvb4[:, ks:ks + 128],
                                 qdbb[:, sl * T + i * 256: sl * T + i * 256 + 256],
                                 start=False, stop=True).then_inc(SEM["boT"], 1)
                        inc("boT")
                    else:
                        n = op[1]
                        h, i = order[n]
                        sl = (h % 2)
                        k_ct = pe_ct_idx[("m", h, i)]
                        if k_ct >= 2:
                            t.wait_ge(SEM["bkv"],
                                      B0["bkv"] + dve_ct_cons[pe_ct_seq[k_ct - 2]])
                        pcb = pcx[k_ct % 2]
                        t.matmul(pcb[:, :],
                                 ktbb[:, sl * T + i * 256: sl * T + i * 256 + 128],
                                 vtbb[:, sl * T + i * 256: sl * T + i * 256 + 128],
                                 start=True, stop=False)
                        t.matmul(pcb[:, :],
                                 ktbb[:, sl * T + i * 256 + 128:
                                      sl * T + i * 256 + 256],
                                 vtbb[:, sl * T + i * 256 + 128:
                                      sl * T + i * 256 + 256],
                                 start=False, stop=True).then_inc(SEM["bct"], 1)
                        inc("bct")

    with ExitStack() as C1:
        wob = C1.enter_context(nc.sbuf_tensor("wob", [128, 16 * D], BF16))
        ocs = [C1.enter_context(nc.sbuf_tensor(f"oc{i}", [128, 16 * 256], BF16))
               for i in range(3)]
        gcbb = [C1.enter_context(nc.sbuf_tensor(f"gcb{i}", [128, 16 * 256], BF16))
                for i in range(2)]
        gts = [C1.enter_context(nc.sbuf_tensor(f"gt{i}", [128, 16 * 256], BF16))
               for i in range(2)]
        sqb = [C1.enter_context(nc.sbuf_tensor(f"sqb{i}", [128, 16 * 256], BF16))
               for i in range(2)]
        rstd1 = C1.enter_context(nc.sbuf_tensor("rstd1", [1, 4 * 256], F32))
        gcb = gcbb
        rbs = [C1.enter_context(nc.sbuf_tensor(f"rb{i}", [128, D], F32))
               for i in range(6)]
        atb = [C1.enter_context(nc.sbuf_tensor(f"at{i}", [128, 1024], F32))
               for i in range(2)]
        lnst = C1.enter_context(nc.sbuf_tensor("lnst", [128, 16], F32))
        normw = C1.enter_context(nc.sbuf_tensor("normw", [128, 16], F32))

        aph = [C1.enter_context(nc.psum_tensor(f"aph{i}", [128, 1024], F32))
               for i in range(2)]
        pss = [C1.enter_context(nc.psum_tensor(f"pss{i}", [1, 256], F32))
               for i in range(2)]
        prp = [C1.enter_context(nc.psum_tensor(f"prp{i}", [128, 256], F32))
               for i in range(2)]

        lo_mark = {c: inc("lo%d" % (c % 3), 16) for c in range(NCH)}
        gg_mark = {c: inc("gg%d" % (c % 2), 16) for c in range(NCH)}
        acc_mark = {ri: inc("acc%d" % (ri % 2), 16) for ri in range(2 * NCH)}
        hs_mark = {ri: inc("hs%d" % (ri % 6),
                           (48 if ri < 2 else 32) if debug else 16)
                   for ri in range(2 * NCH)}
        hb_mark = {ri: inc("hb%d" % (ri % 6), 16) for ri in range(2 * NCH)}
        WOB_MARK = inc("wob_s", 16 * 17)

        with nc.Block() as blk:
            C0 = {k: C[k] for k in
                  ["c1sq", "c1ss", "c1rsa", "c1rsv", "c1rp",
                   "c1g", "c1mm", "c1r", "c1acc", "c1lnm", "c1lsq", "c1var",
                   "c1sd", "c1ap"]}

            @blk.gpsimd
            def _(g):
                for oc in range(16):
                    g.dma_start(out=wob[:, oc * D:(oc + 1) * D],
                                in_=woT[oc * 128:(oc + 1) * 128, :]
                                ).then_inc(SEM["wob_s"], 16)
                g.dma_start(out=normw[:, :], in_=normw_c[:, :]).then_inc(SEM["wob_s"], 16)
                for c in range(min(2, NCH)):
                    g.dma_start(out=gts[c % 2][:, :],
                                in_=gates[c].rearrange("p oc t -> p (oc t)")
                                ).then_inc(SEM["gg%d" % (c % 2)], 16)
                def hsb_store(ri):
                    t0 = (ri // 2) * 256 + (ri % 2) * 128
                    g.wait_ge(SEM["c1ap"], C0["c1ap"] + ri + 1)
                    g.dma_start(out=hsb[t0:t0 + 128, :],
                                in_=rbs[ri % 6][:, :]).then_inc(
                        SEM["hb%d" % (ri % 6)], 16)

                def xload(ri):
                    if ri >= 6:
                        g.wait_ge(SEM["c1ap"], C0["c1ap"] + ri - 5)
                        g.wait_ge(SEM["hs%d" % (ri % 6)], hs_mark[ri - 6])
                        g.wait_ge(SEM["hb%d" % (ri % 6)], hb_mark[ri - 6])
                    t0 = (ri // 2) * 256 + (ri % 2) * 128
                    g.dma_start(out=rbs[ri % 6][:, :],
                                in_=xnat[t0:t0 + 128, :]).then_inc(
                        SEM["acc%d" % (ri % 2)], 16)

                for ri in range(min(4, 2 * NCH)):
                    xload(ri)
                for c in range(NCH):
                    if c >= 1:
                        hsb_store(2 * c - 2)
                        hsb_store(2 * c - 1)
                    if c + 2 < NCH:
                        xload(2 * c + 4)
                        xload(2 * c + 5)
                    if c + 2 < NCH:
                        g.wait_ge(SEM["c1g"], C0["c1g"] + c + 1)
                        g.dma_start(out=gts[(c + 2) % 2][:, :],
                                    in_=gates[c + 2].rearrange("p oc t -> p (oc t)")
                                    ).then_inc(SEM["gg%d" % (c % 2)], 16)
                hsb_store(2 * NCH - 2)
                hsb_store(2 * NCH - 1)
                for smn in ["hb0", "hb1", "hb2", "hb3", "hb4", "hb5"]:
                    g.wait_ge(SEM[smn], C[smn])

            @blk.scalar
            def _(s):
                def sq_only(c):
                    s.wait_ge(SEM["lo%d" % (c % 3)], lo_mark[c])
                    if c >= 2:
                        s.wait_ge(SEM["c1ss"], C0["c1ss"] + c - 1)
                    s.activation(sqb[c % 2][:, :], ocs[c % 3][:, :],
                                 AF.Square).then_inc(SEM["c1sq"], 1)
                    inc("c1sq")

                def sqrt_only(c):
                    s.wait_ge(SEM["c1ss"], C0["c1ss"] + c + 1)
                    s.activation(rstd1[0:1, (c % 4) * 256:(c % 4) * 256 + 256],
                                 pss[c % 2][0:1, :], AF.Sqrt,
                                 bias=cbank[0:1, 1:2], scale=cbank[0:1, 2:3]
                                 ).then_inc(SEM["c1rsa"], 1)
                    inc("c1rsa")

                sq_only(0)
                if NCH > 1:
                    sq_only(1)
                sqrt_only(0)
                if NCH > 2:
                    sq_only(2)
                if NCH > 1:
                    sqrt_only(1)
                for c in range(NCH):
                    if c + 3 < NCH:
                        sq_only(c + 3)
                    if c + 2 < NCH:
                        sqrt_only(c + 2)
                    for ts_ in range(2):
                        ri = c * 2 + ts_
                        for hf in range(2):
                            hi = ri * 2 + hf
                            s.wait_ge(SEM["c1mm"], C0["c1mm"] + hi + 1)
                            if hi >= 2:
                                s.wait_ge(SEM["c1acc"], C0["c1acc"] + hi - 1)
                            s.copy(atb[hi % 2][:, :],
                                   aph[hi % 2][:, :]).then_inc(SEM["c1r"], 1)
                            inc("c1r")
                        s.wait_ge(SEM["c1acc"], C0["c1acc"] + 2 * (ri + 1))
                        s.activation(sqb[c % 2][:, 0:D], rbs[ri % 6][:, :],
                                     AF.Square,
                                     accum_out=lnst[:, 8 * (ri % 2) + 4:
                                                    8 * (ri % 2) + 5]).then_inc(
                            SEM["c1lsq"], 1)
                        inc("c1lsq")
                        s.wait_ge(SEM["c1var"], C0["c1var"] + ri + 1)
                        s.activation(lnst[:, 8 * (ri % 2) + 6: 8 * (ri % 2) + 7],
                                     lnst[:, 8 * (ri % 2) + 3: 8 * (ri % 2) + 4],
                                     AF.Sqrt, bias=cbank[:, 0:1]).then_inc(
                            SEM["c1sd"], 1)
                        inc("c1sd")

            @blk.tensor
            def _(t):
                def ones_mm(c):
                    t.wait_ge(SEM["c1sq"], C0["c1sq"] + c + 1)
                    if c >= 2:
                        t.wait_ge(SEM["c1rsa"], C0["c1rsa"] + c - 1)
                    for oc in range(16):
                        ins = t.matmul(pss[c % 2][0:1, :], onescol_b[:, 0:1],
                                       sqb[c % 2][:, oc * 256:(oc + 1) * 256],
                                       start=(oc == 0), stop=(oc == 15))
                    ins.then_inc(SEM["c1ss"], 1)
                    inc("c1ss")

                def rep_mm(c):
                    t.wait_ge(SEM["c1rsv"], C0["c1rsv"] + c + 1)
                    if c >= 2:
                        t.wait_ge(SEM["c1g"], C0["c1g"] + c - 1)
                    t.matmul(prp[c % 2][:, :], ones1f[0:1, :],
                             rstd1[0:1, (c % 4) * 256:(c % 4) * 256 + 256],
                             start=True, stop=True).then_inc(SEM["c1rp"], 1)
                    inc("c1rp")

                t.wait_ge(SEM["wob_s"], WOB_MARK)
                ones_mm(0)
                if NCH > 1:
                    ones_mm(1)
                rep_mm(0)
                if NCH > 2:
                    ones_mm(2)
                for c in range(NCH):
                    if c + 1 < NCH:
                        rep_mm(c + 1)
                    t.wait_ge(SEM["c1g"], C0["c1g"] + c + 1)
                    for ts_ in range(2):
                        ri = c * 2 + ts_
                        for hf in range(2):
                            hi = ri * 2 + hf
                            if hi >= 2:
                                t.wait_ge(SEM["c1r"], C0["c1r"] + hi - 1)
                            for oc in range(16):
                                for nf in range(2):
                                    ins = t.matmul(
                                        aph[hi % 2][:, nf * 512:(nf + 1) * 512],
                                        gcb[c % 2][:, oc * 256 + ts_ * 128:
                                                   oc * 256 + ts_ * 128 + 128],
                                        wob[:, oc * D + (hf * 2 + nf) * 512:
                                            oc * D + (hf * 2 + nf) * 512 + 512],
                                        start=(oc == 0), stop=(oc == 15))
                            ins.then_inc(SEM["c1mm"], 1)
                            inc("c1mm")
                    if c + 3 < NCH:
                        ones_mm(c + 3)

            @blk.vector
            def _(v):
                def recip_only(c):
                    v.wait_ge(SEM["c1rsa"], C0["c1rsa"] + c + 1)
                    v.reciprocal(rstd1[0:1, (c % 4) * 256:(c % 4) * 256 + 256],
                                 rstd1[0:1, (c % 4) * 256:(c % 4) * 256 + 256]
                                 ).then_inc(SEM["c1rsv"], 1)
                    inc("c1rsv")

                def gate_only(c):
                    v.wait_ge(SEM["gg%d" % (c % 2)], gg_mark[c])
                    v.wait_ge(SEM["c1rp"], C0["c1rp"] + c + 1)
                    if c >= 2:
                        v.wait_ge(SEM["c1mm"], C0["c1mm"] + 4 * (c - 1))
                    for oc in range(16):
                        v.scalar_tensor_tensor(
                            gcb[c % 2][:, oc * 256:(oc + 1) * 256],
                            ocs[c % 3][:, oc * 256:(oc + 1) * 256],
                            normw[:, oc:oc + 1], prp[c % 2][:, :],
                            AOT.mult, AOT.mult)
                        ins_v = v.tensor_tensor(
                            gcb[c % 2][:, oc * 256:(oc + 1) * 256],
                            gcb[c % 2][:, oc * 256:(oc + 1) * 256],
                            gts[c % 2][:, oc * 256:(oc + 1) * 256],
                            AOT.mult)
                    ins_v.then_inc(SEM["c1g"], 1)
                    inc("c1g")

                recip_only(0)
                if NCH > 1:
                    recip_only(1)
                gate_only(0)
                for c in range(NCH):
                    if c + 1 < NCH:
                        gate_only(c + 1)
                    if c + 2 < NCH:
                        recip_only(c + 2)
                    for ts_ in range(2):
                        ri = c * 2 + ts_
                        b8 = 8 * (ri % 2)
                        v.wait_ge(SEM["acc%d" % (ri % 2)], acc_mark[ri])
                        for hf in range(2):
                            hi = ri * 2 + hf
                            v.wait_ge(SEM["c1r"], C0["c1r"] + hi + 1)
                            v.tensor_tensor(
                                rbs[ri % 6][:, hf * 1024:(hf + 1) * 1024],
                                rbs[ri % 6][:, hf * 1024:(hf + 1) * 1024],
                                atb[hi % 2][:, :], AOT.add).then_inc(
                                SEM["c1acc"], 1)
                            inc("c1acc")
                        v.reduce_sum(lnst[:, b8 + 0:b8 + 1], rbs[ri % 6][:, :],
                                     mybir.AxisListType.X)
                        v.tensor_copy(cbank[:, 3:4], cbank[:, 0:1])
                        v.tensor_scalar_mul(lnst[:, b8 + 1:b8 + 2],
                                            lnst[:, b8 + 0:b8 + 1], 1.0 / D)
                        v.tensor_copy(cbank[:, 3:4], cbank[:, 0:1])
                        v.tensor_tensor(lnst[:, b8 + 2:b8 + 3], lnst[:, b8 + 1:b8 + 2],
                                        lnst[:, b8 + 1:b8 + 2], AOT.mult)
                        v.wait_ge(SEM["c1lsq"], C0["c1lsq"] + ri + 1)
                        v.tensor_copy(cbank[:, 3:4], cbank[:, 0:1])
                        v.scalar_tensor_tensor(lnst[:, b8 + 3:b8 + 4],
                                               lnst[:, b8 + 4:b8 + 5], 1.0 / D,
                                               lnst[:, b8 + 2:b8 + 3],
                                               AOT.mult, AOT.subtract).then_inc(
                            SEM["c1var"], 1)
                        inc("c1var")
                        v.wait_ge(SEM["c1sd"], C0["c1sd"] + ri + 1)
                        v.reciprocal(lnst[:, b8 + 7:b8 + 8], lnst[:, b8 + 6:b8 + 7])
                        v.tensor_scalar_sub(rbs[ri % 6][:, :], rbs[ri % 6][:, :],
                                            lnst[:, b8 + 1:b8 + 2])
                        v.tensor_scalar_mul(rbs[ri % 6][:, :], rbs[ri % 6][:, :],
                                            lnst[:, b8 + 7:b8 + 8]).then_inc(
                            SEM["c1ap"], 1)
                        inc("c1ap")

            @blk.sync
            def _(sy):
                def issue_loads(k):
                    sy.dma_start(out=ocs[k % 3][:, :],
                                 in_=outs[k].rearrange("p oc t -> p (oc t)")
                                 ).then_inc(SEM["lo%d" % (k % 3)], 16)

                for k in range(min(4, NCH)):
                    if k >= 3:
                        sy.wait_ge(SEM["c1g"], C0["c1g"] + k - 2)
                    issue_loads(k)
                for c in range(NCH):
                    if c + 4 < NCH:
                        sy.wait_ge(SEM["c1g"], C0["c1g"] + c + 2)
                        issue_loads(c + 4)
                    for ts_ in range(2):
                        ri = c * 2 + ts_
                        sy.wait_ge(SEM["c1ap"], C0["c1ap"] + ri + 1)
                        t0 = c * 256 + ts_ * 128
                        if debug and ri < 2:
                            sy.dma_start(out=d_lnst[:, 8 * ri:8 * ri + 8],
                                         in_=lnst[:, 8 * ri:8 * ri + 8]
                                         ).then_inc(SEM["hs%d" % (ri % 6)], 16)
                        if debug:
                            sy.dma_start(out=d_r[t0:t0 + 128, :],
                                         in_=rbs[ri % 6][:, :]).then_inc(
                                SEM["hs%d" % (ri % 6)], 16)
                        sy.dma_start(out=hsf[t0:t0 + 128, :],
                                     in_=rbs[ri % 6][:, :]).then_inc(
                            SEM["hs%d" % (ri % 6)], 16)
                for smn in ["hs0", "hs1", "hs2", "hs3", "hs4", "hs5"]:
                    sy.wait_ge(SEM[smn], C[smn])

    with ExitStack() as C2:
        wfs = C2.enter_context(nc.sbuf_tensor("wfs", [128, 16 * D], BF16))
        wfw = [C2.enter_context(nc.sbuf_tensor(f"wfw{i}", [128, D], F32))
               for i in range(2)]
        screp = C2.enter_context(nc.sbuf_tensor("screp", [128, D], F32))
        bias_b = C2.enter_context(nc.sbuf_tensor("bias_b", [1, D], BF16))
        bias_f = C2.enter_context(nc.sbuf_tensor("bias_f", [1, D], F32))
        htb = [C2.enter_context(nc.sbuf_tensor(f"htb{i}", [128, 16 * 256], BF16))
               for i in range(2)]
        hfb = [C2.enter_context(nc.sbuf_tensor(f"hfb{i}", [128, D], F32))
               for i in range(2)]
        r2b = [C2.enter_context(nc.sbuf_tensor(f"r2b{i}", [128, D], F32))
               for i in range(2)]
        yb = [C2.enter_context(nc.sbuf_tensor(f"yb{i}", [128, D], F32))
              for i in range(2)]
        sqdmp2 = C2.enter_context(nc.sbuf_tensor("sqdmp2", [128, D], BF16))
        lnst2 = C2.enter_context(nc.sbuf_tensor("lnst2", [128, 16], F32))
        fp = [C2.enter_context(nc.psum_tensor(f"fp{i}", [128, 2048], F32))
              for i in range(2)]

        wf_mark = {ic: inc("wf%d" % (ic % 2), 16) for ic in range(16)}
        SB_MARK = inc("qg", 32)

        tr_mk = {c: inc("ht%d" % (c % 2), 16 * 16) for c in range(NCH)}
        hf_mk = {ri: inc("hf%d" % (ri % 2), 16) for ri in range(2 * NCH)}
        ys_mark = {ri: inc("ys%d" % (ri % 2), 16) for ri in range(2 * NCH)}

        with nc.Block() as blk:
            W0 = {"c2w": C["c2w"]}
            C0 = {k: C[k] for k in ["c2mm", "c2r", "c2lsq",
                                    "c2var", "c2sd", "c2ap"]}

            @blk.scalar
            def _(s):
                for c in range(NCH):
                    for ts_ in range(2):
                        ri = c * 2 + ts_
                        s.wait_ge(SEM["c2r"], C0["c2r"] + ri + 1)
                        s.activation(sqdmp2[:, :], r2b[ri % 2][:, :], AF.Square,
                                     accum_out=lnst2[:, 8 * (ri % 2) + 4:
                                                     8 * (ri % 2) + 5]).then_inc(
                            SEM["c2lsq"], 1)
                        inc("c2lsq")
                        s.wait_ge(SEM["c2var"], C0["c2var"] + ri + 1)
                        s.activation(lnst2[:, 8 * (ri % 2) + 6: 8 * (ri % 2) + 7],
                                     lnst2[:, 8 * (ri % 2) + 3: 8 * (ri % 2) + 4],
                                     AF.Sqrt, bias=cbank[:, 0:1]).then_inc(
                            SEM["c2sd"], 1)
                        inc("c2sd")

            @blk.gpsimd
            def _(g):
                g.dma_start(out=screp[:, :], in_=scalerep[:, :]).then_inc(SEM["qg"], 16)
                g.dma_start(out=bias_f[0:1, :], in_=biasrow[0:1, :]).then_inc(SEM["qg"], 16)
                for ic in range(16):
                    if ic >= 2:
                        g.wait_ge(SEM["c2w"], W0["c2w"] + ic - 1)
                    g.dma_start(out=wfw[ic % 2][:, :],
                                in_=wfT[ic * 128:(ic + 1) * 128, :]
                                ).then_inc(SEM["wf%d" % (ic % 2)], 16)
                for c in range(NCH):
                    for ts_ in range(2):
                        ri = c * 2 + ts_
                        if ri >= 2:
                            g.wait_ge(SEM["c2r"], C0["c2r"] + ri - 1)
                        t0 = c * 256 + ts_ * 128
                        g.dma_start(out=hfb[ri % 2][:, :],
                                    in_=hsf[t0:t0 + 128, :]).then_inc(
                            SEM["hf%d" % (ri % 2)], 16)

            @blk.tensor
            def _(t):
                t.wait_ge(SEM["c2w"], W0["c2w"] + 16)
                for c in range(NCH):
                    t.wait_ge(SEM["ht%d" % (c % 2)], tr_mk[c])
                    for ts_ in range(2):
                        ri = c * 2 + ts_
                        if ri >= 2:
                            t.wait_ge(SEM["c2r"], C0["c2r"] + ri - 1)
                        pfb = fp[ri % 2]
                        for k in range(16):
                            for nf in range(4):
                                t.matmul(
                                    pfb[:, nf * 512:(nf + 1) * 512],
                                    htb[c % 2][:, k * 256 + ts_ * 128:
                                               k * 256 + ts_ * 128 + 128],
                                    wfs[:, k * D + nf * 512: k * D + nf * 512 + 512],
                                    start=(k == 0), stop=False)
                        for nf in range(4):
                            ins = t.matmul(pfb[:, nf * 512:(nf + 1) * 512],
                                           ones1b[0:1, :],
                                           bias_b[0:1, nf * 512:(nf + 1) * 512],
                                           start=False, stop=True)
                        ins.then_inc(SEM["c2mm"], 1)
                        inc("c2mm")

            @blk.vector
            def _(v):
                v.wait_ge(SEM["qg"], SB_MARK)
                v.tensor_copy(bias_b[0:1, :], bias_f[0:1, :])
                for ic in range(16):
                    v.wait_ge(SEM["wf%d" % (ic % 2)], wf_mark[ic])
                    v.tensor_tensor(wfs[:, ic * D:(ic + 1) * D], wfw[ic % 2][:, :],
                                    screp[:, :], AOT.mult).then_inc(SEM["c2w"], 1)
                    inc("c2w")
                for c in range(NCH):
                    for ts_ in range(2):
                        ri = c * 2 + ts_
                        b8 = 8 * (ri % 2)
                        v.wait_ge(SEM["c2mm"], C0["c2mm"] + ri + 1)
                        v.wait_ge(SEM["hf%d" % (ri % 2)], hf_mk[ri])
                        if ri >= 2:
                            v.wait_ge(SEM["ys%d" % (ri % 2)], ys_mark[ri - 2])
                        v.tensor_tensor(r2b[ri % 2][:, :], fp[ri % 2][:, :],
                                        hfb[ri % 2][:, :], AOT.add).then_inc(
                            SEM["c2r"], 1)
                        inc("c2r")
                        v.reduce_sum(lnst2[:, b8 + 0:b8 + 1], r2b[ri % 2][:, :],
                                     mybir.AxisListType.X)
                        v.tensor_copy(cbank[:, 3:4], cbank[:, 0:1])
                        v.tensor_scalar_mul(lnst2[:, b8 + 1:b8 + 2],
                                            lnst2[:, b8 + 0:b8 + 1], 1.0 / D)
                        v.tensor_copy(cbank[:, 3:4], cbank[:, 0:1])
                        v.tensor_tensor(lnst2[:, b8 + 2:b8 + 3],
                                        lnst2[:, b8 + 1:b8 + 2],
                                        lnst2[:, b8 + 1:b8 + 2], AOT.mult)
                        v.wait_ge(SEM["c2lsq"], C0["c2lsq"] + ri + 1)
                        v.tensor_copy(cbank[:, 3:4], cbank[:, 0:1])
                        v.scalar_tensor_tensor(lnst2[:, b8 + 3:b8 + 4],
                                               lnst2[:, b8 + 4:b8 + 5], 1.0 / D,
                                               lnst2[:, b8 + 2:b8 + 3],
                                               AOT.mult, AOT.subtract).then_inc(
                            SEM["c2var"], 1)
                        inc("c2var")
                        v.wait_ge(SEM["c2sd"], C0["c2sd"] + ri + 1)
                        v.reciprocal(lnst2[:, b8 + 7:b8 + 8],
                                     lnst2[:, b8 + 6:b8 + 7])
                        v.tensor_scalar_sub(yb[ri % 2][:, :], r2b[ri % 2][:, :],
                                            lnst2[:, b8 + 1:b8 + 2])
                        v.tensor_scalar_mul(yb[ri % 2][:, :], yb[ri % 2][:, :],
                                            lnst2[:, b8 + 7:b8 + 8]).then_inc(
                            SEM["c2ap"], 1)
                        inc("c2ap")

            @blk.sync
            def _(sy):
                def issue_tr(c):
                    for ic in range(16):
                        sy.dma_start_transpose(
                            htb[c % 2][:, ic * 256:(ic + 1) * 256],
                            hsb[c * 256:(c + 1) * 256, ic * 128:(ic + 1) * 128]
                        ).then_inc(SEM["ht%d" % (c % 2)], 16)

                issue_tr(0)
                if NCH > 1:
                    issue_tr(1)
                for c in range(NCH):
                    if c + 2 < NCH:
                        sy.wait_ge(SEM["c2mm"], C0["c2mm"] + 2 * (c + 1))
                        issue_tr(c + 2)
                    for ts_ in range(2):
                        ri = c * 2 + ts_
                        sy.wait_ge(SEM["c2ap"], C0["c2ap"] + ri + 1)
                        t0 = c * 256 + ts_ * 128
                        sy.dma_start(out=y[t0:t0 + 128, :],
                                     in_=yb[ri % 2][:, :]).then_inc(
                            SEM["ys%d" % (ri % 2)], 16)
                sy.wait_ge(SEM["ys0"], C["ys0"])
                sy.wait_ge(SEM["ys1"], C["ys1"])

    if debug:
        dbgs = [("d_qs", qs, [D, T], BF16), ("d_kts", kts, [NH, T, HD], BF16),
                ("d_vs", vs, [NH, T, HD], BF16), ("d_kps", kps, [NH, TP, HD], BF16),
                ("d_vps", vps, [NH, TP, HD], BF16), ("d_gates", gates, [D, T], F32),
                ("d_outs", outs, [D, T], BF16), ("d_hsf", hsf, [T, D], F32)]
        with nc.semaphore("dbg") as dsem, nc.Block() as blk:
            @blk.sync
            def _(sy):
                n = 0
                for name, srct, shape, dt_ in dbgs:
                    dst = nc.dram_tensor(name, shape, dt_, kind="ExternalOutput")
                    sy.dma_start(out=dst.ap(), in_=srct.ap()).then_inc(dsem, 16)
                    n += 16
                sy.wait_ge(dsem, n)

    es.close()
    return nc



def _prep_inputs(x, slope_rate, w_qkv, w_gate, w_out, norm_w, w_ffn, scale_w,
                 bias, T):
    f32 = np.float32
    Bsz, N, _ = x.shape
    NCORE = (Bsz * N) // T
    chunks = x.reshape(NCORE, T, D)

    idx_q = np.concatenate([np.arange(h * 384, h * 384 + 128) for h in range(16)])
    idx_k = idx_q + 128
    idx_v = idx_q + 256
    wqT = np.ascontiguousarray(w_qkv[idx_q].T.astype(f32))
    wkT = np.ascontiguousarray(w_qkv[idx_k].T.astype(f32))
    wvT = np.ascontiguousarray(w_qkv[idx_v].T.astype(f32))
    wgT = np.ascontiguousarray(w_gate.T.astype(f32))
    woT = np.ascontiguousarray(w_out.T.astype(f32))
    wfT = np.ascontiguousarray(w_ffn.T.astype(f32))

    normw_c = np.ascontiguousarray(norm_w.reshape(16, 128).T.astype(f32))
    scalerep = np.ascontiguousarray(
        np.broadcast_to(scale_w.astype(f32)[None, :], (128, D)))
    biasrow = np.ascontiguousarray(bias.astype(f32).reshape(1, D))
    sloperow = np.ascontiguousarray(slope_rate.astype(f32).reshape(1, 16))

    m = np.arange(256, dtype=f32)
    row256 = (m + 1.0).reshape(1, 256)
    p = np.arange(128, dtype=f32)
    kcol2 = np.stack([255.0 - p, 255.0 - (128.0 + p)], axis=1).astype(f32)
    NPRE = T // 256
    WB = window_blocks(NPRE)
    kcolpre = np.full((128, 16 * 32), 1e9, f32)
    for h in range(16):
        for tc in range(2 * WB[h]):
            n = 128.0 * (tc % 2) + p
            j = tc // 2
            kcolpre[:, h * 32 + tc] = (255.0 - n) + 256.0 * (WB[h] - 1 - j)
    idxTc = np.empty((128, 512), f32)
    maskTc = np.empty((128, 512), f32)
    for half in range(2):
        n = 128.0 * half + p
        idxTc[:, half * 256:(half + 1) * 256] = m[None, :] - n[:, None]
        maskTc[:, half * 256:(half + 1) * 256] = (m[None, :] >= n[:, None])

    zeros_prev = np.zeros((D, T), f32)
    in_maps = []
    for c in range(NCORE):
        b, half = c // 2, c % 2
        xc = chunks[c]
        xTc = np.ascontiguousarray(xc.T)
        if half == 1:
            xprevT = np.ascontiguousarray(chunks[c - 1].T)
        else:
            xprevT = zeros_prev
        in_maps.append({
            "xT": xTc, "xprevT": xprevT, "xnat": np.ascontiguousarray(xc),
            "wqT": wqT, "wkT": wkT, "wvT": wvT, "wgT": wgT, "woT": woT,
            "wfT": wfT, "normw_c": normw_c, "scalerep": scalerep,
            "biasrow": biasrow, "sloperow": sloperow, "row256": row256,
            "kcol2": kcol2, "kcolpre": kcolpre, "idxTc": idxTc,
            "maskTc": maskTc,
        })
    return in_maps


def kernel(L0_attn_hidden_states, L0_attn_slope_rate, L0_attn_w_qkv,
           L0_attn_w_gate, L0_attn_w_out, L0_attn_norm_w, L0_ffn_weight,
           L0_ffn_scale_weight, L0_ffn_bias):
    T = 4096
    x = np.asarray(L0_attn_hidden_states, dtype=np.float32)
    in_maps = _prep_inputs(
        x, np.asarray(L0_attn_slope_rate), np.asarray(L0_attn_w_qkv),
        np.asarray(L0_attn_w_gate), np.asarray(L0_attn_w_out),
        np.asarray(L0_attn_norm_w), np.asarray(L0_ffn_weight),
        np.asarray(L0_ffn_scale_weight), np.asarray(L0_ffn_bias), T)
    nc = build_nc(T)
    res = run_bass_kernel_spmd(nc, in_maps, core_ids=list(range(8)))
    out = np.concatenate([res.results[c]["y"][None] for c in range(8)], axis=0)
    return out.reshape(4, 8192, 2048)


# version 26
# speedup vs baseline: 1.0130x; 1.0130x over previous
import numpy as np
import ml_dtypes
import concourse.bass as bass
import concourse.mybir as mybir
from concourse.bass_utils import run_bass_kernel_spmd
from contextlib import ExitStack

F32 = mybir.dt.float32
BF16 = mybir.dt.bfloat16
AOT = mybir.AluOpType
AF = mybir.ActivationFunctionType

NH, HD, BLK = 16, 128, 256
D = 2048
EPS_RMS = 1e-6
EPS_LN = 1e-5


def window_blocks(npre_max):
    import math
    out = []
    for h in range(NH):
        s = 1.0 / 2.0 ** (8.0 * (h + 1) / 16.0)
        out.append(min(npre_max, max(1, math.ceil(120.0 / (256.0 * s)))))
    return out


def build_nc(T, debug=False, expose=False):
    NB = T // BLK
    TP = T
    NPRE = NB
    NPTC = TP // 128
    NCH = T // 256
    WB = window_blocks(NPRE)
    WTC = [2 * w for w in WB]
    WT0 = [TP - 256 * w for w in WB]

    nc = bass.Bass("TRN2", target_bir_lowering=False)

    xT = nc.dram_tensor("xT", [D, T], F32, kind="ExternalInput")
    xprevT = nc.dram_tensor("xprevT", [D, TP], F32, kind="ExternalInput")
    xnat = nc.dram_tensor("xnat", [T, D], F32, kind="ExternalInput")
    wqT = nc.dram_tensor("wqT", [D, D], F32, kind="ExternalInput")
    wkT = nc.dram_tensor("wkT", [D, D], F32, kind="ExternalInput")
    wvT = nc.dram_tensor("wvT", [D, D], F32, kind="ExternalInput")
    wgT = nc.dram_tensor("wgT", [D, D], F32, kind="ExternalInput")
    woT = nc.dram_tensor("woT", [D, D], F32, kind="ExternalInput")
    wfT = nc.dram_tensor("wfT", [D, D], F32, kind="ExternalInput")
    normw_c = nc.dram_tensor("normw_c", [128, 16], F32, kind="ExternalInput")
    scalerep = nc.dram_tensor("scalerep", [128, D], F32, kind="ExternalInput")
    biasrow = nc.dram_tensor("biasrow", [1, D], F32, kind="ExternalInput")
    sloperow = nc.dram_tensor("sloperow", [1, 16], F32, kind="ExternalInput")
    row256 = nc.dram_tensor("row256", [1, 256], F32, kind="ExternalInput")
    kcol2 = nc.dram_tensor("kcol2", [128, 2], F32, kind="ExternalInput")
    kcolpre = nc.dram_tensor("kcolpre", [128, NH * 32], F32, kind="ExternalInput")
    idxTc = nc.dram_tensor("idxTc", [128, 512], F32, kind="ExternalInput")
    maskTc = nc.dram_tensor("maskTc", [128, 512], F32, kind="ExternalInput")

    y = nc.dram_tensor("y", [T, D], F32, kind="ExternalOutput")

    qs = nc.dram_tensor("qs", [D, T], BF16, kind="Internal")
    kts = nc.dram_tensor("kts", [NH, T, HD], BF16, kind="Internal")
    vs = nc.dram_tensor("vs", [NH, 128, T], BF16, kind="Internal")
    kps = nc.dram_tensor("kps", [NH, 128, TP], BF16, kind="Internal")
    vps = nc.dram_tensor("vps", [NH, 128, TP], BF16, kind="Internal")
    gates = nc.dram_tensor("gates", [T // 256, 128, 16, 256], F32, kind="Internal")
    outs = nc.dram_tensor("outs", [T // 256, 128, 16, 256], BF16,
                      kind="ExternalOutput" if expose else "Internal")
    qdec_d = nc.dram_tensor("qdec_d", [16, 256], F32, kind="Internal")
    hsf = nc.dram_tensor("hsf", [T, D], F32,
                     kind="ExternalOutput" if expose else "Internal")
    hsb = nc.dram_tensor("hsb", [T, D], BF16,
                     kind="ExternalOutput" if expose else "Internal")
    if debug:
        d_r = nc.dram_tensor("d_r", [T, D], F32, kind="ExternalOutput")
        d_lnst = nc.dram_tensor("d_lnst", [128, 16], F32, kind="ExternalOutput")

    es = ExitStack()
    SEM = {}
    C = {}

    def sem(*names):
        for n in names:
            SEM[n] = es.enter_context(nc.semaphore(n))
            C[n] = 0

    def inc(n, k=1):
        C[n] += k
        return C[n]

    sem("w0", "w1", "st0", "st1", "st2", "st3", "qx",
        "lq0", "lq1", "lp", "tr0", "tr1", "os0", "os1", "kt0", "kt1",
        "lo0", "lo1", "lo2", "gg0", "gg1", "acc0", "acc1",
        "hs0", "hs1", "hs2", "hs3", "hs4", "hs5",
        "hb0", "hb1", "hb2", "hb3", "hb4", "hb5", "wob_s",
        "wf0", "wf1", "ht0", "ht1", "hf0", "hf1", "ys0", "ys1",
        "qg", "qs_", "qc", "misc",
        "mmT", "evT", "psc", "mmD", "evD",
        "bkq", "bakq", "bkqm", "boT", "baoT", "bct", "bkv", "bvq",
        "c1sq", "c1ss", "c1rsa", "c1rsv", "c1rp", "c1g", "c1mm", "c1r",
        "c1acc", "c1lnm", "c1lsq", "c1var", "c1sd", "c1ap",
        "c2w", "c2mm", "c2r", "c2lnm", "c2lsq", "c2var", "c2sd", "c2ap",
        "yst")

    ones1f = es.enter_context(nc.sbuf_tensor("ones1f", [1, 128], F32))
    onescol_b = es.enter_context(nc.sbuf_tensor("onescol_b", [128, 1], BF16))
    ones1b = es.enter_context(nc.sbuf_tensor("ones1b", [1, 128], BF16))
    slr = es.enter_context(nc.sbuf_tensor("slr", [1, 16], F32))
    cbank = es.enter_context(nc.sbuf_tensor("cbank", [128, 4], F32))
    sneg_rep = es.enter_context(nc.sbuf_tensor("sneg_rep", [128, 16], F32))
    kdpre_w = es.enter_context(nc.sbuf_tensor("kdpre_w", [128, NH * 32], F32))
    psm_es = ExitStack()
    p_small = psm_es.enter_context(nc.psum_tensor("p_small", [128, 256], F32))
    p_small2 = psm_es.enter_context(nc.psum_tensor("p_small2", [128, 256], F32))

    with nc.Block():
        pass

    with nc.Block() as blk:
        @blk.gpsimd
        def _(g):
            g.dma_start(out=slr[0:1, :], in_=sloperow[0:1, :]).then_inc(SEM["qg"], 16)
            inc("qg", 16)
            g.memset(ones1f[0:1, :], 1.0)
            g.memset(ones1b[0:1, :], 1.0)
            g.memset(onescol_b[:, 0:1], 1.0)
            g.memset(cbank[:, 0:1], float(EPS_LN))
            g.memset(cbank[:, 1:2], float(EPS_RMS))
            g.memset(cbank[:, 2:3], float(1.0 / D))
            g.wait_ge(SEM["qg"], C["qg"])

    with nc.Block() as blk:
        @blk.vector
        def _(v):
            v.tensor_scalar_mul(slr[0:1, :], slr[0:1, :], -1.0)

    with nc.Block() as blk:
        @blk.tensor
        def _(t):
            t.matmul(p_small[:, 0:16], ones1f[0:1, :], slr[0:1, :],
                     start=True, stop=True)

    with nc.Block() as blk:
        @blk.scalar
        def _(s):
            s.copy(sneg_rep[:, :], p_small[:, 0:16])

    kcol_stage = es.enter_context(nc.sbuf_tensor("kcol_stage", [128, NH * 32], F32))
    with nc.Block() as blk:
        @blk.gpsimd
        def _(g):
            g.dma_start(out=kcol_stage[:, :], in_=kcolpre[:, :]).then_inc(SEM["qg"], 16)
            inc("qg", 16)
            g.wait_ge(SEM["qg"], C["qg"])

    with nc.Block() as blk:
        @blk.vector
        def _(v):
            v.tensor_tensor(
                kdpre_w[:, :].rearrange("p (h tc) -> p h tc", h=NH),
                sneg_rep[:, :].unsqueeze(2).broadcast_to([128, NH, 32]),
                kcol_stage[:, :].rearrange("p (h tc) -> p h tc", h=NH),
                AOT.mult)

    with nc.Block() as blk:
        @blk.scalar
        def _(s):
            s.activation(kdpre_w[:, :], kdpre_w[:, :], AF.Exp)

    def emit_projT(blk, xbuf, xlen, groups, prescale, psT, stagT, wkvb):
        mm0, ev0, ps0 = C["mmT"], C["evT"], C["psc"]
        ng = len(groups)
        cum = [0]
        for g_ in groups:
            cum.append(cum[-1] + g_[7])
        wmark = [inc("w%d" % (gi % 2), 32) for gi in range(ng)]
        smark = [inc("st%d" % (gi % 2), 32) for gi in range(ng)]

        @blk.gpsimd
        def _(g):
            for gi, grp in enumerate(groups):
                wA, cA, wB, cB = grp[0], grp[1], grp[2], grp[3]
                if gi >= 2:
                    g.wait_ge(SEM["mmT"], mm0 + cum[gi - 1])
                sl = (gi % 2) * (16 * 256)
                wv = wkvb[:, sl:sl + 16 * 256].rearrange("p (a c) -> p a c", a=16)
                g.dma_start(out=wv[:, :, 0:128],
                            in_=wA[:, cA:cA + 128].rearrange("(a p) c -> p a c", p=128)
                            ).then_inc(SEM["w%d" % (gi % 2)], 16)
                g.dma_start(out=wv[:, :, 128:256],
                            in_=wB[:, cB:cB + 128].rearrange("(a p) c -> p a c", p=128)
                            ).then_inc(SEM["w%d" % (gi % 2)], 16)

        @blk.tensor
        def _(t):
            for gi, grp in enumerate(groups):
                tok0, ntc = grp[6], grp[7]
                t.wait_ge(SEM["w%d" % (gi % 2)], wmark[gi])
                sl = (gi % 2) * (16 * 256)
                for tc in range(ntc):
                    gidx = cum[gi] + tc
                    if gidx >= 2:
                        t.wait_ge(SEM["evT"], ev0 + gidx - 1)
                    pt = psT[gidx % 2]
                    for k in range(16):
                        ins = t.matmul(
                            pt[:, :],
                            xbuf[:, k * xlen + tok0 + tc * 128:
                                 k * xlen + tok0 + tc * 128 + 128],
                            wkvb[:, sl + k * 256: sl + k * 256 + 256],
                            start=(k == 0), stop=(k == 15))
                    ins.then_inc(SEM["mmT"], 1)
                    inc("mmT")

        @blk.scalar
        def _(s):
            for gi, grp in enumerate(groups):
                ntc = grp[7]
                for tc in range(ntc):
                    gidx = cum[gi] + tc
                    s.wait_ge(SEM["mmT"], mm0 + gidx + 1)
                    if gi >= 2 and tc == 0:
                        s.wait_ge(SEM["st%d" % (gi % 2)], smark[gi - 2])
                    st = stagT[gi % 2]
                    s.activation(st[:, tc * 256:(tc + 1) * 256],
                                 psT[gidx % 2][:, :], AF.Silu).then_inc(SEM["evT"], 1)
                    inc("evT")

        if prescale is not None:
            @blk.vector
            def _(v):
                for gi, grp in enumerate(groups):
                    h = prescale[gi]
                    ntc = grp[7]
                    st = stagT[gi % 2]
                    for tc in range(ntc):
                        gidx = cum[gi] + tc
                        v.wait_ge(SEM["evT"], ev0 + gidx + 1)
                        v.tensor_scalar_mul(
                            st[:, tc * 256: tc * 256 + 128],
                            st[:, tc * 256: tc * 256 + 128],
                            kdpre_w[:, h * 32 + tc: h * 32 + tc + 1],
                        ).then_inc(SEM["psc"], 1)
                        inc("psc")

        @blk.sync
        def _(sy):
            for gi, grp in enumerate(groups):
                spillA, spillB, ntc = grp[4], grp[5], grp[7]
                if prescale is not None:
                    sy.wait_ge(SEM["psc"], ps0 + cum[gi + 1])
                else:
                    sy.wait_ge(SEM["evT"], ev0 + cum[gi + 1])
                st = stagT[gi % 2]
                stv = st[:, 0:ntc * 256].rearrange("p (tc c) -> p tc c", c=256)

                def spill_out(sp):
                    if sp.shape[0] == 128:
                        return sp[:, 0:ntc * 128].rearrange(
                            "p (tc d) -> p tc d", d=128)
                    return sp[0:ntc * 128].rearrange("(tc p) d -> p tc d", p=128)

                sy.dma_start(out=spill_out(spillA),
                             in_=stv[:, :, 0:128]).then_inc(SEM["st%d" % (gi % 2)], 16)
                sy.dma_start(out=spill_out(spillB),
                             in_=stv[:, :, 128:256]).then_inc(SEM["st%d" % (gi % 2)], 16)
            sy.wait_ge(SEM["st0"], C["st0"])
            sy.wait_ge(SEM["st1"], C["st1"])

    def emit_projD(blk, xbuf, xlen, mcs, psD, evb, evf, wdb, xload=None):
        ng = xlen // 512
        mm0, ev0, qx0 = C["mmD"], C["evD"], C["qx"]
        nm = len(mcs)
        wmark = [inc("w%d" % (gi % 2), 16) for gi in range(nm)]
        smark = [inc("st%d" % (gidx % 4), 16) for gidx in range(nm * ng)]

        def xslice(g, n):
            g.dma_start(
                out=xbuf[:, 0:16 * xlen].rearrange("p (fc t) -> p fc t",
                                                   t=xlen)[:, :,
                                                           n * 512:(n + 1) * 512],
                in_=xload.rearrange("(fc p) t -> p fc t",
                                    p=128)[:, :, n * 512:(n + 1) * 512]
            ).then_inc(SEM["qx"], 16)
            inc("qx", 16)

        @blk.gpsimd
        def _(g):
            for gi, (wsrc, c0, _sp, _a, _dt) in enumerate(mcs):
                if gi >= 2:
                    g.wait_ge(SEM["mmD"], mm0 + ng * (gi - 1))
                sl = (gi % 2) * (16 * 128)
                g.dma_start(out=wdb[:, sl:sl + 16 * 128].rearrange(
                                "p (a c) -> p a c", a=16),
                            in_=wsrc[:, c0:c0 + 128].rearrange("(a p) c -> p a c", p=128)
                            ).then_inc(SEM["w%d" % (gi % 2)], 16)
                if xload is not None:
                    if gi == 0:
                        for n in range(min(2, ng)):
                            xslice(g, n)
                    elif gi == 1:
                        for n in range(2, ng):
                            xslice(g, n)
            if xload is not None:
                g.wait_ge(SEM["qx"], C["qx"])

        @blk.tensor
        def _(t):
            for gi in range(nm):
                t.wait_ge(SEM["w%d" % (gi % 2)], wmark[gi])
                sl = (gi % 2) * (16 * 128)
                for n in range(ng):
                    gidx = gi * ng + n
                    if xload is not None and gi == 0:
                        t.wait_ge(SEM["qx"], qx0 + (n + 1) * 16)
                    if gidx >= 2:
                        t.wait_ge(SEM["evD"], ev0 + gidx - 1)
                    pd = psD[gidx % 2]
                    for k in range(16):
                        ins = t.matmul(
                            pd[:, :],
                            wdb[:, sl + k * 128: sl + k * 128 + 128],
                            xbuf[:, k * xlen + n * 512: k * xlen + n * 512 + 512],
                            start=(k == 0), stop=(k == 15))
                    ins.then_inc(SEM["mmD"], 1)
                    inc("mmD")

        @blk.scalar
        def _(s):
            for gi, (_w, _c0, _sp, act, odt) in enumerate(mcs):
                for n in range(ng):
                    gidx = gi * ng + n
                    s.wait_ge(SEM["mmD"], mm0 + gidx + 1)
                    if gidx >= 4:
                        s.wait_ge(SEM["st%d" % (gidx % 4)], smark[gidx - 4])
                    ev = evb[gidx % 4] if odt == BF16 else evf[gidx % 4]
                    s.activation(ev[:, :], psD[gidx % 2][:, :], act).then_inc(
                        SEM["evD"], 1)
                    inc("evD")

        @blk.sync
        def _(sy):
            for gi, (_w, c0, spill, _a, odt) in enumerate(mcs):
                for n in range(ng):
                    gidx = gi * ng + n
                    sy.wait_ge(SEM["evD"], ev0 + gidx + 1)
                    ev = evb[gidx % 4] if odt == BF16 else evf[gidx % 4]
                    if spill is None or len(spill.shape) == 4:
                        sy.dma_start(
                            out=spill[2 * n:2 * n + 2, :, c0 // 128, :]
                            .rearrange("c p t -> p c t"),
                            in_=ev[:, :].rearrange("p (c t) -> p c t", c=2)
                        ).then_inc(SEM["st%d" % (gidx % 4)], 16)
                    else:
                        sy.dma_start(out=spill[c0:c0 + 128, n * 512:(n + 1) * 512],
                                     in_=ev[:, :]).then_inc(
                            SEM["st%d" % (gidx % 4)], 16)
            for i in range(4):
                sy.wait_ge(SEM["st%d" % i], C["st%d" % i])

    with ExitStack() as A:
        xbuf = A.enter_context(nc.sbuf_tensor("xbuf", [128, 16 * T], BF16))
        wkvb = A.enter_context(nc.sbuf_tensor("wkvb", [128, 2 * 16 * 256], BF16))
        wdb = A.enter_context(nc.sbuf_tensor("wdb", [128, 2 * 16 * 128], BF16))
        stg0 = A.enter_context(nc.sbuf_tensor("stg0", [128, (T // 128) * 256], BF16))
        stg1 = A.enter_context(nc.sbuf_tensor("stg1", [128, (T // 128) * 256], BF16))
        evb = [A.enter_context(nc.sbuf_tensor(f"evb{i}", [128, 512], BF16))
               for i in range(4)]
        evf = [A.enter_context(nc.sbuf_tensor(f"evf{i}", [128, 512], F32))
               for i in range(4)]
        psT = [A.enter_context(nc.psum_tensor(f"psT{i}", [128, 256], F32))
               for i in range(2)]
        psD = [A.enter_context(nc.psum_tensor(f"psD{i}", [128, 512], F32))
               for i in range(2)]
        stagT = [stg0, stg1]

        with nc.Block() as blk:
            @blk.gpsimd
            def _(g):
                for fc in range(16):
                    g.dma_start(out=xbuf[:, fc * TP:(fc + 1) * TP],
                                in_=xprevT[fc * 128:(fc + 1) * 128, :]
                                ).then_inc(SEM["qx"], 16)
                    inc("qx", 16)
                g.wait_ge(SEM["qx"], C["qx"])

        with nc.Block() as blk:
            emit_projT(blk, xbuf, TP,
                       [(wkT, h * 128, wvT, h * 128, kps[h], vps[h],
                         WT0[h], WTC[h]) for h in range(NH)],
                       prescale=list(range(NH)), psT=psT, stagT=stagT, wkvb=wkvb)

        with nc.Block() as blk:
            emit_projD(blk, xbuf, T,
                       [(wqT, mc * 128, qs, AF.Silu, BF16) for mc in range(16)],
                       psD=psD, evb=evb, evf=evf, wdb=wdb, xload=xT)
        with nc.Block() as blk:
            emit_projT(blk, xbuf, T,
                       [(wkT, h * 128, wvT, h * 128, kts[h], vs[h],
                         0, T // 128) for h in range(NH)],
                       prescale=None, psT=psT, stagT=stagT, wkvb=wkvb)
        with nc.Block() as blk:
            emit_projD(blk, xbuf, T,
                       [(wgT, mc * 128, gates, AF.Sigmoid, F32) for mc in range(16)],
                       psD=psD, evb=evb, evf=evf, wdb=wdb)

    with ExitStack() as B:
        qdecr = B.enter_context(nc.sbuf_tensor("qdecr", [128, NH * 256], F32))
        kdTt = B.enter_context(nc.sbuf_tensor("kdTt", [128, NH * 2], F32))
        bdr = B.enter_context(nc.sbuf_tensor("bdr", [128, NH], F32))
        ddt = B.enter_context(nc.sbuf_tensor("ddt", [128, NH * 512], BF16))
        tw0 = B.enter_context(nc.sbuf_tensor("tw0", [128, 512], F32))
        tw1 = B.enter_context(nc.sbuf_tensor("tw1", [128, 512], F32))
        tw2 = B.enter_context(nc.sbuf_tensor("tw2", [128, 512], F32))
        r16 = B.enter_context(nc.sbuf_tensor("r16", [16, 257], F32))
        qrow = B.enter_context(nc.sbuf_tensor("qrow", [1, 16 * 256], F32))
        qbb = B.enter_context(nc.sbuf_tensor("qbb", [128, 2 * T], BF16))
        kdbb = B.enter_context(nc.sbuf_tensor("kdbb", [128, 2 * T], BF16))
        ktbb = B.enter_context(nc.sbuf_tensor("ktbb", [128, 2 * T], BF16))
        vtbb = B.enter_context(nc.sbuf_tensor("vtbb", [128, 2 * T], BF16))
        kpbb = B.enter_context(nc.sbuf_tensor("kpbb", [128, TP], BF16))
        vpbb = B.enter_context(nc.sbuf_tensor("vpbb", [128, TP], BF16))
        qdbb = B.enter_context(nc.sbuf_tensor("qdbb", [128, 2 * T], BF16))
        kqmb = B.enter_context(nc.sbuf_tensor("kqmb", [128, 4 * 512], BF16))
        kqeb = B.enter_context(nc.sbuf_tensor("kqeb", [128, 4 * 512], BF16))
        kvf2 = B.enter_context(nc.sbuf_tensor("kvf2", [128, 2 * 128], F32))
        kvb4 = B.enter_context(nc.sbuf_tensor("kvb4", [128, 4 * 128], BF16))
        ostg0 = B.enter_context(nc.sbuf_tensor("ostg0", [128, T], BF16))
        ostg1 = B.enter_context(nc.sbuf_tensor("ostg1", [128, T], BF16))
        ostg = [ostg0, ostg1]

        with nc.Block() as blk:
            @blk.gpsimd
            def _(g):
                g.dma_start(out=tw0[:, 0:2], in_=kcol2[:, :]).then_inc(SEM["qg"], 16)
                inc("qg", 16)
                g.dma_start(out=tw1[:, 0:512], in_=idxTc[:, :]).then_inc(SEM["qg"], 16)
                inc("qg", 16)
                g.dma_start(out=tw2[:, 0:512], in_=maskTc[:, :]).then_inc(SEM["qg"], 16)
                inc("qg", 16)
                g.wait_ge(SEM["qg"], C["qg"])

            @blk.sync
            def _(sy):
                sy.dma_start(out=r16[0:16, 256:257],
                             in_=sloperow[0:1, :].rearrange("one h -> h one")
                             ).then_inc(SEM["qs_"], 16)
                inc("qs_", 16)
                sy.wait_ge(SEM["qs_"], C["qs_"])

        with nc.Block() as blk:
            @blk.vector
            def _(v):
                v.tensor_tensor(
                    kdTt[:, :].rearrange("p (h t) -> p h t", h=NH),
                    sneg_rep[:, :].unsqueeze(2).broadcast_to([128, NH, 2]),
                    tw0[:, 0:2].unsqueeze(1).broadcast_to([128, NH, 2]),
                    AOT.mult)
                v.tensor_scalar_mul(bdr[:, :], sneg_rep[:, :], 256.0)
                v.tensor_scalar_mul(r16[0:16, 256:257], r16[0:16, 256:257], -1.0)

        with nc.Block() as blk:
            @blk.scalar
            def _(s):
                s.activation(kdTt[:, :], kdTt[:, :], AF.Exp)
                s.activation(bdr[:, :], bdr[:, :], AF.Exp)

        with nc.Block() as blk:
            @blk.gpsimd
            def _(g):
                g.dma_start(out=qrow[0:1, 0:256], in_=row256[0:1, :]).then_inc(SEM["qg"], 16)
                inc("qg", 16)
                g.wait_ge(SEM["qg"], C["qg"])

        with nc.Block() as blk:
            @blk.tensor
            def _(t):
                t.matmul(p_small[0:16, 0:256], ones1f[0:1, 0:16], qrow[0:1, 0:256],
                         start=True, stop=True)

        with nc.Block() as blk:
            @blk.scalar
            def _(s):
                s.activation(r16[0:16, 0:256], p_small[0:16, 0:256], AF.Exp,
                             scale=r16[0:16, 256:257])

        with nc.Block() as blk:
            @blk.sync
            def _(sy):
                sy.dma_start(out=qdec_d[:, :],
                             in_=r16[0:16, 0:256]).then_inc(SEM["qs_"], 16)
                inc("qs_", 16)
                sy.wait_ge(SEM["qs_"], C["qs_"])

        with nc.Block() as blk:
            @blk.gpsimd
            def _(g):
                g.dma_start(out=qrow[0:1, :],
                            in_=qdec_d[:, :].rearrange("h m -> (h m)")[None, :]
                            ).then_inc(SEM["qg"], 16)
                inc("qg", 16)
                g.wait_ge(SEM["qg"], C["qg"])

        with nc.Block() as blk:
            m0 = C["misc"]

            @blk.tensor
            def _(t):
                for h in range(NH):
                    if h >= 2:
                        t.wait_ge(SEM["misc"], m0 + h - 1)
                    ps = p_small if h % 2 == 0 else p_small2
                    t.matmul(ps[:, 0:256], ones1f[0:1, :],
                             qrow[0:1, h * 256:(h + 1) * 256],
                             start=True, stop=True).then_inc(SEM["bkq"], 1)
                    inc("bkq")

            @blk.scalar
            def _(s):
                for h in range(NH):
                    s.wait_ge(SEM["bkq"], C["bkq"] - NH + h + 1)
                    ps = p_small if h % 2 == 0 else p_small2
                    s.copy(qdecr[:, h * 256:(h + 1) * 256], ps[:, 0:256]).then_inc(
                        SEM["misc"], 1)
                    inc("misc")

        for h in range(NH):
            with nc.Block() as blk:
                @blk.vector
                def _(v, h=h):
                    v.tensor_scalar(tw0[:, 0:512], tw1[:, 0:512],
                                    sneg_rep[:, h:h + 1], 0.0,
                                    AOT.mult, AOT.min)

                @blk.scalar
                def _(s, h=h):
                    pass
            with nc.Block() as blk:
                @blk.scalar
                def _(s, h=h):
                    s.activation(tw0[:, 0:512], tw0[:, 0:512], AF.Exp)
            with nc.Block() as blk:
                @blk.vector
                def _(v, h=h):
                    v.tensor_tensor(ddt[:, h * 512:(h + 1) * 512], tw0[:, 0:512],
                                    tw2[:, 0:512], AOT.mult)

        psm_es.close()
        pkq = [B.enter_context(nc.psum_tensor(f"pkq{i}", [128, 512], F32))
               for i in range(3)]
        po = [B.enter_context(nc.psum_tensor(f"po{i}", [128, 256], F32))
              for i in range(3)]
        pcx = [B.enter_context(nc.psum_tensor(f"pc{i}", [128, 128], F32))
               for i in range(2)]

        HB = max(NB // 2, 1)
        order = []
        for p in range(NH + 1):
            for j in range(HB):
                for (h_, i_) in ((p - 1, j + HB), (p, j)):
                    if 0 <= h_ < NH and i_ < NB:
                        order.append((h_, i_))
        NOR = len(order)
        assert NOR == NH * NB
        pos = {hi: n for n, hi in enumerate(order)}
        LA = 2

        pe_stream = []
        _touched = set()

        def _pe_touch(m):
            hh, ii = order[m]
            if ii == 0 and hh not in _touched:
                _touched.add(hh)
                pe_stream.append(("ldw", hh))
            pe_stream.append(("kq", m))

        for m in range(min(LA, NOR)):
            _pe_touch(m)
        for n in range(NOR):
            if n + LA < NOR:
                _pe_touch(n + LA)
            h_, i_ = order[n]
            if i_ == 0:
                pe_stream.append(("pre", h_))
            pe_stream.append(("oT", n))
            if i_ < NB - 1:
                pe_stream.append(("ct", n))

        pe_ct_seq = []
        for op in pe_stream:
            if op[0] == "pre":
                pe_ct_seq.append(("pre", op[1]))
            elif op[0] == "ct":
                pe_ct_seq.append(("m",) + order[op[1]])
        pe_ct_idx = {cid: k for k, cid in enumerate(pe_ct_seq)}

        dve_stream = []
        _kqm_emitted = 0
        for n in range(NOR):
            h_, i_ = order[n]
            if i_ == 0:
                dve_stream.append(("prep", h_))
                dve_stream.append(("kvinit", h_))
            while _kqm_emitted <= min(n + 2, NOR - 1):
                dve_stream.append(("kqm", _kqm_emitted))
                _kqm_emitted += 1
            if i_ < NB - 1:
                dve_stream.append(("upd", n))
        dve_kv_ready = {}
        dve_ct_cons = {}
        _bkv = 0
        for op in dve_stream:
            if op[0] == "kvinit":
                _bkv += 1
                dve_kv_ready[(op[1], 0)] = _bkv
                dve_ct_cons[("pre", op[1])] = _bkv
            elif op[0] == "upd":
                h_, i_ = order[op[1]]
                _bkv += 1
                dve_kv_ready[(h_, i_ + 1)] = _bkv
                dve_ct_cons[("m", h_, i_)] = _bkv

        B0 = {k: C[k] for k in ["bkq", "bakq", "bkqm",
                                "boT", "baoT", "bct", "bkv", "bvq"]}
        ld_mark = {h: inc("lq%d" % (h % 2), 32) for h in range(NH)}
        gp_mark = {h: inc("lp", 32) for h in range(NH)}
        tr_mark = {h: inc("tr%d" % (h % 2), 16) for h in range(NH)}
        kt_mark = {h: inc("kt%d" % (h % 2), 16) for h in range(NH)}
        st_mark = {h: inc("os%d" % (h % 2), 16) for h in range(NH)}
        sync_ops = [("T", 0), ("K", 0), ("T", 1), ("K", 1)]
        for h in range(NH):
            if h + 2 < NH:
                sync_ops.append(("T", h + 2))
                sync_ops.append(("K", h + 2))
            sync_ops.append(("S", h))

        def kvslot(h, i):
            return ((h % 2) * 2 + (i % 2)) * 128

        with nc.Block() as blk:
            @blk.scalar
            def _(s):
                def issue_loads(h):
                    sl = (h % 2)
                    sm = SEM["lq%d" % sl]
                    s.dma_start(out=qbb[:, sl * T:sl * T + T],
                                in_=qs[h * 128:(h + 1) * 128, :]).then_inc(sm, 16)
                    s.dma_start(out=vtbb[:, sl * T:sl * T + T],
                                in_=vs[h]).then_inc(sm, 16)

                issue_loads(0)
                issue_loads(1)

                def kqe(m):
                    s.wait_ge(SEM["bkq"], B0["bkq"] + m + 1)
                    if m >= 4:
                        s.wait_ge(SEM["bkqm"], B0["bkqm"] + m - 3)
                    s.copy(kqeb[:, (m % 4) * 512:(m % 4) * 512 + 512],
                           pkq[m % 3][:, :]).then_inc(SEM["bakq"], 1)
                    inc("bakq")

                for m0 in range(min(2, NOR)):
                    kqe(m0)
                for n in range(NOR):
                    if n + 2 < NOR:
                        kqe(n + 2)
                    h, i = order[n]
                    s.wait_ge(SEM["boT"], B0["boT"] + n + 1)
                    if i == 0 and h >= 2:
                        s.wait_ge(SEM["os%d" % (h % 2)], st_mark[h - 2])
                    s.copy(ostg[h % 2][:, i * 256:(i + 1) * 256],
                           po[n % 3][:, :]).then_inc(SEM["baoT"], 1)
                    inc("baoT")
                    if i == NB - 1 and h + 2 < NH:
                        issue_loads(h + 2)

            @blk.gpsimd
            def _(g):
                for h in range(NH):
                    if h >= 1:
                        g.wait_ge(SEM["bct"],
                                  B0["bct"] + pe_ct_idx[("pre", h - 1)] + 1)
                    g.dma_start(out=kpbb[:, 0:WTC[h] * 128],
                                in_=kps[h][:, 0:WTC[h] * 128]).then_inc(SEM["lp"], 16)
                    g.dma_start(out=vpbb[:, 0:WTC[h] * 128],
                                in_=vps[h][:, 0:WTC[h] * 128]).then_inc(SEM["lp"], 16)

            @blk.sync
            def _(sy):
                for op, h in sync_ops:
                    if op == "T":
                        if h >= 2:
                            sy.wait_ge(SEM["bkq"],
                                       B0["bkq"] + pos[(h - 2, NB - 1)] + 1)
                        sy.dma_start_transpose(kdbb[:, (h % 2) * T:(h % 2) * T + T],
                                               kts[h]).then_inc(
                            SEM["tr%d" % (h % 2)], 16)
                    elif op == "K":
                        if h >= 2:
                            sy.wait_ge(SEM["boT"],
                                       B0["boT"] + pos[(h - 2, NB - 1)] + 1)
                        sy.dma_start(out=ktbb[:, (h % 2) * T:(h % 2) * T + T]
                                     .rearrange("p (tc d) -> p tc d", d=128),
                                     in_=kts[h].rearrange("(tc p) d -> p tc d", p=128)
                                     ).then_inc(SEM["kt%d" % (h % 2)], 16)
                    else:
                        sy.wait_ge(SEM["baoT"], B0["baoT"] + pos[(h, NB - 1)] + 1)
                        sy.dma_start(out=outs[:, :, h, :].rearrange("c p t -> p c t"),
                                     in_=ostg[h % 2][:, :].rearrange(
                                         "p (c t) -> p c t", t=256)).then_inc(
                            SEM["os%d" % (h % 2)], 16)
                for smn in ["tr0", "tr1", "os0", "os1", "kt0", "kt1"]:
                    sy.wait_ge(SEM[smn], C[smn])

            @blk.vector
            def _(v):
                for op in dve_stream:
                    if op[0] == "prep":
                        h = op[1]
                        sl = h % 2
                        v.wait_ge(SEM["lq%d" % sl], ld_mark[h])
                        v.wait_ge(SEM["kt%d" % sl], kt_mark[h])
                        if h >= 2:
                            v.wait_ge(SEM["boT"],
                                      B0["boT"] + pos[(h - 2, NB - 1)] + 1)
                        v.tensor_tensor(
                            qdbb[:, sl * T:sl * T + T].rearrange(
                                "p (r m) -> p r m", m=256),
                            qbb[:, sl * T:sl * T + T].rearrange(
                                "p (r m) -> p r m", m=256),
                            qdecr[:, h * 256:(h + 1) * 256].unsqueeze(1)
                            .broadcast_to([128, NB, 256]),
                            AOT.mult)
                        for half in range(2):
                            ins_v = v.tensor_tensor(
                                ktbb[:, sl * T:sl * T + T].rearrange(
                                    "p (tcp two d) -> p tcp two d", two=2, d=128
                                )[:, :, half, :],
                                ktbb[:, sl * T:sl * T + T].rearrange(
                                    "p (tcp two d) -> p tcp two d", two=2, d=128
                                )[:, :, half, :],
                                kdTt[:, h * 2 + half:h * 2 + half + 1]
                                .unsqueeze(2).broadcast_to([128, NB, 128]),
                                AOT.mult)
                        ins_v.then_inc(SEM["bvq"], 1)
                        inc("bvq")
                    elif op[0] == "kvinit":
                        h = op[1]
                        k_ct = pe_ct_idx[("pre", h)]
                        v.wait_ge(SEM["bct"], B0["bct"] + k_ct + 1)
                        v.tensor_copy(kvf2[:, (h % 2) * 128:(h % 2) * 128 + 128],
                                      pcx[k_ct % 2][:, :])
                        v.tensor_copy(kvb4[:, kvslot(h, 0):kvslot(h, 0) + 128],
                                      kvf2[:, (h % 2) * 128:(h % 2) * 128 + 128]
                                      ).then_inc(SEM["bkv"], 1)
                        inc("bkv")
                    elif op[0] == "kqm":
                        m = op[1]
                        h, i = order[m]
                        v.wait_ge(SEM["bakq"], B0["bakq"] + m + 1)
                        if m >= 4:
                            v.wait_ge(SEM["boT"], B0["boT"] + m - 3)
                        v.tensor_tensor(kqmb[:, (m % 4) * 512:(m % 4) * 512 + 512],
                                        kqeb[:, (m % 4) * 512:(m % 4) * 512 + 512],
                                        ddt[:, h * 512:(h + 1) * 512],
                                        AOT.mult).then_inc(SEM["bkqm"], 1)
                        inc("bkqm")
                    else:
                        n = op[1]
                        h, i = order[n]
                        k_ct = pe_ct_idx[("m", h, i)]
                        v.wait_ge(SEM["bct"], B0["bct"] + k_ct + 1)
                        if i >= 1:
                            v.wait_ge(SEM["boT"], B0["boT"] + pos[(h, i - 1)] + 1)
                        kf = kvf2[:, (h % 2) * 128:(h % 2) * 128 + 128]
                        v.scalar_tensor_tensor(kf, kf, bdr[:, h:h + 1],
                                               pcx[k_ct % 2][:, :],
                                               AOT.mult, AOT.add)
                        sl2 = kvslot(h, i + 1)
                        v.tensor_copy(kvb4[:, sl2:sl2 + 128], kf).then_inc(
                            SEM["bkv"], 1)
                        inc("bkv")

            @blk.tensor
            def _(t):
                for op in pe_stream:
                    if op[0] == "ldw":
                        h = op[1]
                        t.wait_ge(SEM["lq%d" % (h % 2)], ld_mark[h])
                        t.wait_ge(SEM["tr%d" % (h % 2)], tr_mark[h])
                        t.wait_ge(SEM["kt%d" % (h % 2)], kt_mark[h])
                    elif op[0] == "kq":
                        m = op[1]
                        h, i = order[m]
                        sl = (h % 2)
                        if m >= 3:
                            t.wait_ge(SEM["bakq"], B0["bakq"] + m - 2)
                        pk = pkq[m % 3]
                        t.matmul(pk[:, 0:256],
                                 kdbb[:, sl * T + i * 256: sl * T + i * 256 + 128],
                                 qbb[:, sl * T + i * 256: sl * T + i * 256 + 256],
                                 start=True, stop=True)
                        t.matmul(pk[:, 256:512],
                                 kdbb[:, sl * T + i * 256 + 128:
                                      sl * T + i * 256 + 256],
                                 qbb[:, sl * T + i * 256: sl * T + i * 256 + 256],
                                 start=True, stop=True).then_inc(SEM["bkq"], 1)
                        inc("bkq")
                    elif op[0] == "pre":
                        h = op[1]
                        t.wait_ge(SEM["bvq"], B0["bvq"] + h + 1)
                        t.wait_ge(SEM["lp"], gp_mark[h])
                        k_ct = pe_ct_idx[("pre", h)]
                        if k_ct >= 2:
                            t.wait_ge(SEM["bkv"],
                                      B0["bkv"] + dve_ct_cons[pe_ct_seq[k_ct - 2]])
                        pcb = pcx[k_ct % 2]
                        for ptc in range(WTC[h]):
                            ins = t.matmul(
                                pcb[:, :],
                                kpbb[:, ptc * 128: ptc * 128 + 128],
                                vpbb[:, ptc * 128: ptc * 128 + 128],
                                start=(ptc == 0), stop=(ptc == WTC[h] - 1))
                        ins.then_inc(SEM["bct"], 1)
                        inc("bct")
                    elif op[0] == "oT":
                        n = op[1]
                        h, i = order[n]
                        sl = (h % 2)
                        if n >= 3:
                            t.wait_ge(SEM["baoT"], B0["baoT"] + n - 2)
                        t.wait_ge(SEM["bkqm"], B0["bkqm"] + n + 1)
                        t.wait_ge(SEM["bkv"], B0["bkv"] + dve_kv_ready[(h, i)])
                        pob = po[n % 3]
                        sl4 = (n % 4) * 512
                        ks = kvslot(h, i)
                        t.matmul(pob[:, :],
                                 vtbb[:, sl * T + i * 256: sl * T + i * 256 + 128],
                                 kqmb[:, sl4:sl4 + 256], start=True, stop=False)
                        t.matmul(pob[:, :],
                                 vtbb[:, sl * T + i * 256 + 128:
                                      sl * T + i * 256 + 256],
                                 kqmb[:, sl4 + 256:sl4 + 512],
                                 start=False, stop=False)
                        t.matmul(pob[:, :],
                                 kvb4[:, ks:ks + 128],
                                 qdbb[:, sl * T + i * 256: sl * T + i * 256 + 256],
                                 start=False, stop=True).then_inc(SEM["boT"], 1)
                        inc("boT")
                    else:
                        n = op[1]
                        h, i = order[n]
                        sl = (h % 2)
                        k_ct = pe_ct_idx[("m", h, i)]
                        if k_ct >= 2:
                            t.wait_ge(SEM["bkv"],
                                      B0["bkv"] + dve_ct_cons[pe_ct_seq[k_ct - 2]])
                        pcb = pcx[k_ct % 2]
                        t.matmul(pcb[:, :],
                                 ktbb[:, sl * T + i * 256: sl * T + i * 256 + 128],
                                 vtbb[:, sl * T + i * 256: sl * T + i * 256 + 128],
                                 start=True, stop=False)
                        t.matmul(pcb[:, :],
                                 ktbb[:, sl * T + i * 256 + 128:
                                      sl * T + i * 256 + 256],
                                 vtbb[:, sl * T + i * 256 + 128:
                                      sl * T + i * 256 + 256],
                                 start=False, stop=True).then_inc(SEM["bct"], 1)
                        inc("bct")

    with ExitStack() as C1:
        wob = C1.enter_context(nc.sbuf_tensor("wob", [128, 16 * D], BF16))
        ocs = [C1.enter_context(nc.sbuf_tensor(f"oc{i}", [128, 16 * 256], BF16))
               for i in range(3)]
        gcbb = [C1.enter_context(nc.sbuf_tensor(f"gcb{i}", [128, 16 * 256], BF16))
                for i in range(2)]
        gts = [C1.enter_context(nc.sbuf_tensor(f"gt{i}", [128, 16 * 256], BF16))
               for i in range(2)]
        sqb = [C1.enter_context(nc.sbuf_tensor(f"sqb{i}", [128, 16 * 256], BF16))
               for i in range(2)]
        rstd1 = C1.enter_context(nc.sbuf_tensor("rstd1", [1, 4 * 256], F32))
        gcb = gcbb
        rbs = [C1.enter_context(nc.sbuf_tensor(f"rb{i}", [128, D], F32))
               for i in range(6)]
        atb = [C1.enter_context(nc.sbuf_tensor(f"at{i}", [128, 1024], F32))
               for i in range(2)]
        lnst = C1.enter_context(nc.sbuf_tensor("lnst", [128, 16], F32))
        normw = C1.enter_context(nc.sbuf_tensor("normw", [128, 16], F32))

        aph = [C1.enter_context(nc.psum_tensor(f"aph{i}", [128, 1024], F32))
               for i in range(2)]
        pss = [C1.enter_context(nc.psum_tensor(f"pss{i}", [1, 256], F32))
               for i in range(2)]
        prp = [C1.enter_context(nc.psum_tensor(f"prp{i}", [128, 256], F32))
               for i in range(2)]

        lo_mark = {c: inc("lo%d" % (c % 3), 16) for c in range(NCH)}
        gg_mark = {c: inc("gg%d" % (c % 2), 16) for c in range(NCH)}
        acc_mark = {ri: inc("acc%d" % (ri % 2), 16) for ri in range(2 * NCH)}
        hs_mark = {ri: inc("hs%d" % (ri % 6),
                           (48 if ri < 2 else 32) if debug else 16)
                   for ri in range(2 * NCH)}
        hb_mark = {ri: inc("hb%d" % (ri % 6), 16) for ri in range(2 * NCH)}
        WOB_MARK = inc("wob_s", 16 * 17)

        with nc.Block() as blk:
            C0 = {k: C[k] for k in
                  ["c1sq", "c1ss", "c1rsa", "c1rsv", "c1rp",
                   "c1g", "c1mm", "c1r", "c1acc", "c1lnm", "c1lsq", "c1var",
                   "c1sd", "c1ap"]}

            @blk.gpsimd
            def _(g):
                for oc in range(16):
                    g.dma_start(out=wob[:, oc * D:(oc + 1) * D],
                                in_=woT[oc * 128:(oc + 1) * 128, :]
                                ).then_inc(SEM["wob_s"], 16)
                g.dma_start(out=normw[:, :], in_=normw_c[:, :]).then_inc(SEM["wob_s"], 16)
                for c in range(min(2, NCH)):
                    g.dma_start(out=gts[c % 2][:, :],
                                in_=gates[c].rearrange("p oc t -> p (oc t)")
                                ).then_inc(SEM["gg%d" % (c % 2)], 16)
                def hsb_store(ri):
                    t0 = (ri // 2) * 256 + (ri % 2) * 128
                    g.wait_ge(SEM["c1ap"], C0["c1ap"] + ri + 1)
                    g.dma_start(out=hsb[t0:t0 + 128, :],
                                in_=rbs[ri % 6][:, :]).then_inc(
                        SEM["hb%d" % (ri % 6)], 16)

                def xload(ri):
                    if ri >= 6:
                        g.wait_ge(SEM["c1ap"], C0["c1ap"] + ri - 5)
                        g.wait_ge(SEM["hs%d" % (ri % 6)], hs_mark[ri - 6])
                        g.wait_ge(SEM["hb%d" % (ri % 6)], hb_mark[ri - 6])
                    t0 = (ri // 2) * 256 + (ri % 2) * 128
                    g.dma_start(out=rbs[ri % 6][:, :],
                                in_=xnat[t0:t0 + 128, :]).then_inc(
                        SEM["acc%d" % (ri % 2)], 16)

                for ri in range(min(4, 2 * NCH)):
                    xload(ri)
                for c in range(NCH):
                    if c >= 1:
                        hsb_store(2 * c - 2)
                        hsb_store(2 * c - 1)
                    if c + 2 < NCH:
                        xload(2 * c + 4)
                        xload(2 * c + 5)
                    if c + 2 < NCH:
                        g.wait_ge(SEM["c1g"], C0["c1g"] + c + 1)
                        g.dma_start(out=gts[(c + 2) % 2][:, :],
                                    in_=gates[c + 2].rearrange("p oc t -> p (oc t)")
                                    ).then_inc(SEM["gg%d" % (c % 2)], 16)
                hsb_store(2 * NCH - 2)
                hsb_store(2 * NCH - 1)
                for smn in ["hb0", "hb1", "hb2", "hb3", "hb4", "hb5"]:
                    g.wait_ge(SEM[smn], C[smn])

            @blk.scalar
            def _(s):
                def sq_only(c):
                    s.wait_ge(SEM["lo%d" % (c % 3)], lo_mark[c])
                    if c >= 2:
                        s.wait_ge(SEM["c1ss"], C0["c1ss"] + c - 1)
                    s.activation(sqb[c % 2][:, :], ocs[c % 3][:, :],
                                 AF.Square).then_inc(SEM["c1sq"], 1)
                    inc("c1sq")

                def sqrt_only(c):
                    s.wait_ge(SEM["c1ss"], C0["c1ss"] + c + 1)
                    s.activation(rstd1[0:1, (c % 4) * 256:(c % 4) * 256 + 256],
                                 pss[c % 2][0:1, :], AF.Sqrt,
                                 bias=cbank[0:1, 1:2], scale=cbank[0:1, 2:3]
                                 ).then_inc(SEM["c1rsa"], 1)
                    inc("c1rsa")

                sq_only(0)
                if NCH > 1:
                    sq_only(1)
                sqrt_only(0)
                if NCH > 2:
                    sq_only(2)
                if NCH > 1:
                    sqrt_only(1)
                for c in range(NCH):
                    if c + 3 < NCH:
                        sq_only(c + 3)
                    if c + 2 < NCH:
                        sqrt_only(c + 2)
                    for ts_ in range(2):
                        ri = c * 2 + ts_
                        for hf in range(2):
                            hi = ri * 2 + hf
                            s.wait_ge(SEM["c1mm"], C0["c1mm"] + hi + 1)
                            if hi >= 2:
                                s.wait_ge(SEM["c1acc"], C0["c1acc"] + hi - 1)
                            s.copy(atb[hi % 2][:, :],
                                   aph[hi % 2][:, :]).then_inc(SEM["c1r"], 1)
                            inc("c1r")
                        s.wait_ge(SEM["c1acc"], C0["c1acc"] + 2 * (ri + 1))
                        s.activation(sqb[c % 2][:, 0:D], rbs[ri % 6][:, :],
                                     AF.Square,
                                     accum_out=lnst[:, 8 * (ri % 2) + 4:
                                                    8 * (ri % 2) + 5]).then_inc(
                            SEM["c1lsq"], 1)
                        inc("c1lsq")
                        s.wait_ge(SEM["c1var"], C0["c1var"] + ri + 1)
                        s.activation(lnst[:, 8 * (ri % 2) + 6: 8 * (ri % 2) + 7],
                                     lnst[:, 8 * (ri % 2) + 3: 8 * (ri % 2) + 4],
                                     AF.Sqrt, bias=cbank[:, 0:1]).then_inc(
                            SEM["c1sd"], 1)
                        inc("c1sd")

            @blk.tensor
            def _(t):
                def ones_mm(c):
                    t.wait_ge(SEM["c1sq"], C0["c1sq"] + c + 1)
                    if c >= 2:
                        t.wait_ge(SEM["c1rsa"], C0["c1rsa"] + c - 1)
                    for oc in range(16):
                        ins = t.matmul(pss[c % 2][0:1, :], onescol_b[:, 0:1],
                                       sqb[c % 2][:, oc * 256:(oc + 1) * 256],
                                       start=(oc == 0), stop=(oc == 15))
                    ins.then_inc(SEM["c1ss"], 1)
                    inc("c1ss")

                def rep_mm(c):
                    t.wait_ge(SEM["c1rsv"], C0["c1rsv"] + c + 1)
                    if c >= 2:
                        t.wait_ge(SEM["c1g"], C0["c1g"] + c - 1)
                    t.matmul(prp[c % 2][:, :], ones1f[0:1, :],
                             rstd1[0:1, (c % 4) * 256:(c % 4) * 256 + 256],
                             start=True, stop=True).then_inc(SEM["c1rp"], 1)
                    inc("c1rp")

                t.wait_ge(SEM["wob_s"], WOB_MARK)
                ones_mm(0)
                if NCH > 1:
                    ones_mm(1)
                rep_mm(0)
                if NCH > 2:
                    ones_mm(2)
                for c in range(NCH):
                    if c + 1 < NCH:
                        rep_mm(c + 1)
                    t.wait_ge(SEM["c1g"], C0["c1g"] + c + 1)
                    for ts_ in range(2):
                        ri = c * 2 + ts_
                        for hf in range(2):
                            hi = ri * 2 + hf
                            if hi >= 2:
                                t.wait_ge(SEM["c1r"], C0["c1r"] + hi - 1)
                            for oc in range(16):
                                for nf in range(2):
                                    ins = t.matmul(
                                        aph[hi % 2][:, nf * 512:(nf + 1) * 512],
                                        gcb[c % 2][:, oc * 256 + ts_ * 128:
                                                   oc * 256 + ts_ * 128 + 128],
                                        wob[:, oc * D + (hf * 2 + nf) * 512:
                                            oc * D + (hf * 2 + nf) * 512 + 512],
                                        start=(oc == 0), stop=(oc == 15))
                            ins.then_inc(SEM["c1mm"], 1)
                            inc("c1mm")
                    if c + 3 < NCH:
                        ones_mm(c + 3)

            @blk.vector
            def _(v):
                def recip_only(c):
                    v.wait_ge(SEM["c1rsa"], C0["c1rsa"] + c + 1)
                    v.reciprocal(rstd1[0:1, (c % 4) * 256:(c % 4) * 256 + 256],
                                 rstd1[0:1, (c % 4) * 256:(c % 4) * 256 + 256]
                                 ).then_inc(SEM["c1rsv"], 1)
                    inc("c1rsv")

                def gate_only(c):
                    v.wait_ge(SEM["gg%d" % (c % 2)], gg_mark[c])
                    v.wait_ge(SEM["c1rp"], C0["c1rp"] + c + 1)
                    if c >= 2:
                        v.wait_ge(SEM["c1mm"], C0["c1mm"] + 4 * (c - 1))
                    for oc in range(16):
                        v.scalar_tensor_tensor(
                            gcb[c % 2][:, oc * 256:(oc + 1) * 256],
                            ocs[c % 3][:, oc * 256:(oc + 1) * 256],
                            normw[:, oc:oc + 1], prp[c % 2][:, :],
                            AOT.mult, AOT.mult)
                        ins_v = v.tensor_tensor(
                            gcb[c % 2][:, oc * 256:(oc + 1) * 256],
                            gcb[c % 2][:, oc * 256:(oc + 1) * 256],
                            gts[c % 2][:, oc * 256:(oc + 1) * 256],
                            AOT.mult)
                    ins_v.then_inc(SEM["c1g"], 1)
                    inc("c1g")

                recip_only(0)
                if NCH > 1:
                    recip_only(1)
                gate_only(0)
                for c in range(NCH):
                    if c + 1 < NCH:
                        gate_only(c + 1)
                    if c + 2 < NCH:
                        recip_only(c + 2)
                    for ts_ in range(2):
                        ri = c * 2 + ts_
                        b8 = 8 * (ri % 2)
                        v.wait_ge(SEM["acc%d" % (ri % 2)], acc_mark[ri])
                        for hf in range(2):
                            hi = ri * 2 + hf
                            v.wait_ge(SEM["c1r"], C0["c1r"] + hi + 1)
                            v.tensor_tensor(
                                rbs[ri % 6][:, hf * 1024:(hf + 1) * 1024],
                                rbs[ri % 6][:, hf * 1024:(hf + 1) * 1024],
                                atb[hi % 2][:, :], AOT.add).then_inc(
                                SEM["c1acc"], 1)
                            inc("c1acc")
                        v.reduce_sum(lnst[:, b8 + 0:b8 + 1], rbs[ri % 6][:, :],
                                     mybir.AxisListType.X)
                        v.tensor_copy(cbank[:, 3:4], cbank[:, 0:1])
                        v.tensor_scalar_mul(lnst[:, b8 + 1:b8 + 2],
                                            lnst[:, b8 + 0:b8 + 1], 1.0 / D)
                        v.tensor_copy(cbank[:, 3:4], cbank[:, 0:1])
                        v.tensor_tensor(lnst[:, b8 + 2:b8 + 3], lnst[:, b8 + 1:b8 + 2],
                                        lnst[:, b8 + 1:b8 + 2], AOT.mult)
                        v.wait_ge(SEM["c1lsq"], C0["c1lsq"] + ri + 1)
                        v.tensor_copy(cbank[:, 3:4], cbank[:, 0:1])
                        v.scalar_tensor_tensor(lnst[:, b8 + 3:b8 + 4],
                                               lnst[:, b8 + 4:b8 + 5], 1.0 / D,
                                               lnst[:, b8 + 2:b8 + 3],
                                               AOT.mult, AOT.subtract).then_inc(
                            SEM["c1var"], 1)
                        inc("c1var")
                        v.wait_ge(SEM["c1sd"], C0["c1sd"] + ri + 1)
                        v.reciprocal(lnst[:, b8 + 7:b8 + 8], lnst[:, b8 + 6:b8 + 7])
                        v.tensor_scalar_sub(rbs[ri % 6][:, :], rbs[ri % 6][:, :],
                                            lnst[:, b8 + 1:b8 + 2])
                        v.tensor_scalar_mul(rbs[ri % 6][:, :], rbs[ri % 6][:, :],
                                            lnst[:, b8 + 7:b8 + 8]).then_inc(
                            SEM["c1ap"], 1)
                        inc("c1ap")

            @blk.sync
            def _(sy):
                def issue_loads(k):
                    sy.dma_start(out=ocs[k % 3][:, :],
                                 in_=outs[k].rearrange("p oc t -> p (oc t)")
                                 ).then_inc(SEM["lo%d" % (k % 3)], 16)

                for k in range(min(4, NCH)):
                    if k >= 3:
                        sy.wait_ge(SEM["c1g"], C0["c1g"] + k - 2)
                    issue_loads(k)
                for c in range(NCH):
                    if c + 4 < NCH:
                        sy.wait_ge(SEM["c1g"], C0["c1g"] + c + 2)
                        issue_loads(c + 4)
                    for ts_ in range(2):
                        ri = c * 2 + ts_
                        sy.wait_ge(SEM["c1ap"], C0["c1ap"] + ri + 1)
                        t0 = c * 256 + ts_ * 128
                        if debug and ri < 2:
                            sy.dma_start(out=d_lnst[:, 8 * ri:8 * ri + 8],
                                         in_=lnst[:, 8 * ri:8 * ri + 8]
                                         ).then_inc(SEM["hs%d" % (ri % 6)], 16)
                        if debug:
                            sy.dma_start(out=d_r[t0:t0 + 128, :],
                                         in_=rbs[ri % 6][:, :]).then_inc(
                                SEM["hs%d" % (ri % 6)], 16)
                        sy.dma_start(out=hsf[t0:t0 + 128, :],
                                     in_=rbs[ri % 6][:, :]).then_inc(
                            SEM["hs%d" % (ri % 6)], 16)
                for smn in ["hs0", "hs1", "hs2", "hs3", "hs4", "hs5"]:
                    sy.wait_ge(SEM[smn], C[smn])

    with ExitStack() as C2:
        wfs = C2.enter_context(nc.sbuf_tensor("wfs", [128, 16 * D], BF16))
        wfw = [C2.enter_context(nc.sbuf_tensor(f"wfw{i}", [128, D], F32))
               for i in range(2)]
        screp = C2.enter_context(nc.sbuf_tensor("screp", [128, D], F32))
        bias_b = C2.enter_context(nc.sbuf_tensor("bias_b", [1, D], BF16))
        bias_f = C2.enter_context(nc.sbuf_tensor("bias_f", [1, D], F32))
        htb = [C2.enter_context(nc.sbuf_tensor(f"htb{i}", [128, 16 * 256], BF16))
               for i in range(2)]
        hfb = [C2.enter_context(nc.sbuf_tensor(f"hfb{i}", [128, D], F32))
               for i in range(2)]
        r2b = [C2.enter_context(nc.sbuf_tensor(f"r2b{i}", [128, D], F32))
               for i in range(2)]
        yb = [C2.enter_context(nc.sbuf_tensor(f"yb{i}", [128, D], F32))
              for i in range(2)]
        sqdmp2 = C2.enter_context(nc.sbuf_tensor("sqdmp2", [128, D], BF16))
        lnst2 = C2.enter_context(nc.sbuf_tensor("lnst2", [128, 16], F32))
        fp = [C2.enter_context(nc.psum_tensor(f"fp{i}", [128, 2048], F32))
              for i in range(2)]

        wf_mark = {ic: inc("wf%d" % (ic % 2), 16) for ic in range(16)}
        SB_MARK = inc("qg", 32)

        with nc.Block() as blk:
            W0 = {"c2w": C["c2w"]}

            @blk.gpsimd
            def _(g):
                g.dma_start(out=screp[:, :], in_=scalerep[:, :]).then_inc(SEM["qg"], 16)
                g.dma_start(out=bias_f[0:1, :], in_=biasrow[0:1, :]).then_inc(SEM["qg"], 16)
                for ic in range(16):
                    if ic >= 2:
                        g.wait_ge(SEM["c2w"], W0["c2w"] + ic - 1)
                    g.dma_start(out=wfw[ic % 2][:, :],
                                in_=wfT[ic * 128:(ic + 1) * 128, :]
                                ).then_inc(SEM["wf%d" % (ic % 2)], 16)

            @blk.vector
            def _(v):
                v.wait_ge(SEM["qg"], SB_MARK)
                v.tensor_copy(bias_b[0:1, :], bias_f[0:1, :])
                for ic in range(16):
                    v.wait_ge(SEM["wf%d" % (ic % 2)], wf_mark[ic])
                    v.tensor_tensor(wfs[:, ic * D:(ic + 1) * D], wfw[ic % 2][:, :],
                                    screp[:, :], AOT.mult).then_inc(SEM["c2w"], 1)
                    inc("c2w")

        tr_mk = {c: inc("ht%d" % (c % 2), 16 * 16) for c in range(NCH)}
        hf_mk = {ri: inc("hf%d" % (ri % 2), 16) for ri in range(2 * NCH)}
        ys_mark = {ri: inc("ys%d" % (ri % 2), 16) for ri in range(2 * NCH)}

        with nc.Block() as blk:
            C0 = {k: C[k] for k in ["c2mm", "c2r", "c2lsq",
                                    "c2var", "c2sd", "c2ap"]}

            @blk.scalar
            def _(s):
                for c in range(NCH):
                    for ts_ in range(2):
                        ri = c * 2 + ts_
                        s.wait_ge(SEM["c2r"], C0["c2r"] + ri + 1)
                        s.activation(sqdmp2[:, :], r2b[ri % 2][:, :], AF.Square,
                                     accum_out=lnst2[:, 8 * (ri % 2) + 4:
                                                     8 * (ri % 2) + 5]).then_inc(
                            SEM["c2lsq"], 1)
                        inc("c2lsq")
                        s.wait_ge(SEM["c2var"], C0["c2var"] + ri + 1)
                        s.activation(lnst2[:, 8 * (ri % 2) + 6: 8 * (ri % 2) + 7],
                                     lnst2[:, 8 * (ri % 2) + 3: 8 * (ri % 2) + 4],
                                     AF.Sqrt, bias=cbank[:, 0:1]).then_inc(
                            SEM["c2sd"], 1)
                        inc("c2sd")

            @blk.gpsimd
            def _(g):
                for c in range(NCH):
                    for ts_ in range(2):
                        ri = c * 2 + ts_
                        if ri >= 2:
                            g.wait_ge(SEM["c2r"], C0["c2r"] + ri - 1)
                        t0 = c * 256 + ts_ * 128
                        g.dma_start(out=hfb[ri % 2][:, :],
                                    in_=hsf[t0:t0 + 128, :]).then_inc(
                            SEM["hf%d" % (ri % 2)], 16)

            @blk.tensor
            def _(t):
                t.wait_ge(SEM["c2w"], C["c2w"])
                for c in range(NCH):
                    t.wait_ge(SEM["ht%d" % (c % 2)], tr_mk[c])
                    for ts_ in range(2):
                        ri = c * 2 + ts_
                        if ri >= 2:
                            t.wait_ge(SEM["c2r"], C0["c2r"] + ri - 1)
                        pfb = fp[ri % 2]
                        for k in range(16):
                            for nf in range(4):
                                t.matmul(
                                    pfb[:, nf * 512:(nf + 1) * 512],
                                    htb[c % 2][:, k * 256 + ts_ * 128:
                                               k * 256 + ts_ * 128 + 128],
                                    wfs[:, k * D + nf * 512: k * D + nf * 512 + 512],
                                    start=(k == 0), stop=False)
                        for nf in range(4):
                            ins = t.matmul(pfb[:, nf * 512:(nf + 1) * 512],
                                           ones1b[0:1, :],
                                           bias_b[0:1, nf * 512:(nf + 1) * 512],
                                           start=False, stop=True)
                        ins.then_inc(SEM["c2mm"], 1)
                        inc("c2mm")

            @blk.vector
            def _(v):
                for c in range(NCH):
                    for ts_ in range(2):
                        ri = c * 2 + ts_
                        b8 = 8 * (ri % 2)
                        v.wait_ge(SEM["c2mm"], C0["c2mm"] + ri + 1)
                        v.wait_ge(SEM["hf%d" % (ri % 2)], hf_mk[ri])
                        if ri >= 2:
                            v.wait_ge(SEM["ys%d" % (ri % 2)], ys_mark[ri - 2])
                        v.tensor_tensor(r2b[ri % 2][:, :], fp[ri % 2][:, :],
                                        hfb[ri % 2][:, :], AOT.add).then_inc(
                            SEM["c2r"], 1)
                        inc("c2r")
                        v.reduce_sum(lnst2[:, b8 + 0:b8 + 1], r2b[ri % 2][:, :],
                                     mybir.AxisListType.X)
                        v.tensor_copy(cbank[:, 3:4], cbank[:, 0:1])
                        v.tensor_scalar_mul(lnst2[:, b8 + 1:b8 + 2],
                                            lnst2[:, b8 + 0:b8 + 1], 1.0 / D)
                        v.tensor_copy(cbank[:, 3:4], cbank[:, 0:1])
                        v.tensor_tensor(lnst2[:, b8 + 2:b8 + 3],
                                        lnst2[:, b8 + 1:b8 + 2],
                                        lnst2[:, b8 + 1:b8 + 2], AOT.mult)
                        v.wait_ge(SEM["c2lsq"], C0["c2lsq"] + ri + 1)
                        v.tensor_copy(cbank[:, 3:4], cbank[:, 0:1])
                        v.scalar_tensor_tensor(lnst2[:, b8 + 3:b8 + 4],
                                               lnst2[:, b8 + 4:b8 + 5], 1.0 / D,
                                               lnst2[:, b8 + 2:b8 + 3],
                                               AOT.mult, AOT.subtract).then_inc(
                            SEM["c2var"], 1)
                        inc("c2var")
                        v.wait_ge(SEM["c2sd"], C0["c2sd"] + ri + 1)
                        v.reciprocal(lnst2[:, b8 + 7:b8 + 8],
                                     lnst2[:, b8 + 6:b8 + 7])
                        v.tensor_scalar_sub(yb[ri % 2][:, :], r2b[ri % 2][:, :],
                                            lnst2[:, b8 + 1:b8 + 2])
                        v.tensor_scalar_mul(yb[ri % 2][:, :], yb[ri % 2][:, :],
                                            lnst2[:, b8 + 7:b8 + 8]).then_inc(
                            SEM["c2ap"], 1)
                        inc("c2ap")

            @blk.sync
            def _(sy):
                def issue_tr(c):
                    for ic in range(16):
                        sy.dma_start_transpose(
                            htb[c % 2][:, ic * 256:(ic + 1) * 256],
                            hsb[c * 256:(c + 1) * 256, ic * 128:(ic + 1) * 128]
                        ).then_inc(SEM["ht%d" % (c % 2)], 16)

                issue_tr(0)
                if NCH > 1:
                    issue_tr(1)
                for c in range(NCH):
                    if c + 2 < NCH:
                        sy.wait_ge(SEM["c2mm"], C0["c2mm"] + 2 * (c + 1))
                        issue_tr(c + 2)
                    for ts_ in range(2):
                        ri = c * 2 + ts_
                        sy.wait_ge(SEM["c2ap"], C0["c2ap"] + ri + 1)
                        t0 = c * 256 + ts_ * 128
                        sy.dma_start(out=y[t0:t0 + 128, :],
                                     in_=yb[ri % 2][:, :]).then_inc(
                            SEM["ys%d" % (ri % 2)], 16)
                sy.wait_ge(SEM["ys0"], C["ys0"])
                sy.wait_ge(SEM["ys1"], C["ys1"])

    if debug:
        dbgs = [("d_qs", qs, [D, T], BF16), ("d_kts", kts, [NH, T, HD], BF16),
                ("d_vs", vs, [NH, T, HD], BF16), ("d_kps", kps, [NH, TP, HD], BF16),
                ("d_vps", vps, [NH, TP, HD], BF16), ("d_gates", gates, [D, T], F32),
                ("d_outs", outs, [D, T], BF16), ("d_hsf", hsf, [T, D], F32)]
        with nc.semaphore("dbg") as dsem, nc.Block() as blk:
            @blk.sync
            def _(sy):
                n = 0
                for name, srct, shape, dt_ in dbgs:
                    dst = nc.dram_tensor(name, shape, dt_, kind="ExternalOutput")
                    sy.dma_start(out=dst.ap(), in_=srct.ap()).then_inc(dsem, 16)
                    n += 16
                sy.wait_ge(dsem, n)

    es.close()
    return nc



def _prep_inputs(x, slope_rate, w_qkv, w_gate, w_out, norm_w, w_ffn, scale_w,
                 bias, T):
    f32 = np.float32
    Bsz, N, _ = x.shape
    NCORE = (Bsz * N) // T
    chunks = x.reshape(NCORE, T, D)

    idx_q = np.concatenate([np.arange(h * 384, h * 384 + 128) for h in range(16)])
    idx_k = idx_q + 128
    idx_v = idx_q + 256
    wqT = np.ascontiguousarray(w_qkv[idx_q].T.astype(f32))
    wkT = np.ascontiguousarray(w_qkv[idx_k].T.astype(f32))
    wvT = np.ascontiguousarray(w_qkv[idx_v].T.astype(f32))
    wgT = np.ascontiguousarray(w_gate.T.astype(f32))
    woT = np.ascontiguousarray(w_out.T.astype(f32))
    wfT = np.ascontiguousarray(w_ffn.T.astype(f32))

    normw_c = np.ascontiguousarray(norm_w.reshape(16, 128).T.astype(f32))
    scalerep = np.ascontiguousarray(
        np.broadcast_to(scale_w.astype(f32)[None, :], (128, D)))
    biasrow = np.ascontiguousarray(bias.astype(f32).reshape(1, D))
    sloperow = np.ascontiguousarray(slope_rate.astype(f32).reshape(1, 16))

    m = np.arange(256, dtype=f32)
    row256 = (m + 1.0).reshape(1, 256)
    p = np.arange(128, dtype=f32)
    kcol2 = np.stack([255.0 - p, 255.0 - (128.0 + p)], axis=1).astype(f32)
    NPRE = T // 256
    WB = window_blocks(NPRE)
    kcolpre = np.full((128, 16 * 32), 1e9, f32)
    for h in range(16):
        for tc in range(2 * WB[h]):
            n = 128.0 * (tc % 2) + p
            j = tc // 2
            kcolpre[:, h * 32 + tc] = (255.0 - n) + 256.0 * (WB[h] - 1 - j)
    idxTc = np.empty((128, 512), f32)
    maskTc = np.empty((128, 512), f32)
    for half in range(2):
        n = 128.0 * half + p
        idxTc[:, half * 256:(half + 1) * 256] = m[None, :] - n[:, None]
        maskTc[:, half * 256:(half + 1) * 256] = (m[None, :] >= n[:, None])

    zeros_prev = np.zeros((D, T), f32)
    in_maps = []
    for c in range(NCORE):
        b, half = c // 2, c % 2
        xc = chunks[c]
        xTc = np.ascontiguousarray(xc.T)
        if half == 1:
            xprevT = np.ascontiguousarray(chunks[c - 1].T)
        else:
            xprevT = zeros_prev
        in_maps.append({
            "xT": xTc, "xprevT": xprevT, "xnat": np.ascontiguousarray(xc),
            "wqT": wqT, "wkT": wkT, "wvT": wvT, "wgT": wgT, "woT": woT,
            "wfT": wfT, "normw_c": normw_c, "scalerep": scalerep,
            "biasrow": biasrow, "sloperow": sloperow, "row256": row256,
            "kcol2": kcol2, "kcolpre": kcolpre, "idxTc": idxTc,
            "maskTc": maskTc,
        })
    return in_maps


def kernel(L0_attn_hidden_states, L0_attn_slope_rate, L0_attn_w_qkv,
           L0_attn_w_gate, L0_attn_w_out, L0_attn_norm_w, L0_ffn_weight,
           L0_ffn_scale_weight, L0_ffn_bias):
    T = 4096
    x = np.asarray(L0_attn_hidden_states, dtype=np.float32)
    in_maps = _prep_inputs(
        x, np.asarray(L0_attn_slope_rate), np.asarray(L0_attn_w_qkv),
        np.asarray(L0_attn_w_gate), np.asarray(L0_attn_w_out),
        np.asarray(L0_attn_norm_w), np.asarray(L0_ffn_weight),
        np.asarray(L0_ffn_scale_weight), np.asarray(L0_ffn_bias), T)
    nc = build_nc(T)
    res = run_bass_kernel_spmd(nc, in_maps, core_ids=list(range(8)))
    out = np.concatenate([res.results[c]["y"][None] for c in range(8)], axis=0)
    return out.reshape(4, 8192, 2048)
